# version 14
# baseline (speedup 1.0000x reference)
import numpy as np
from contextlib import ExitStack

import concourse.bass as bass
import concourse.mybir as mybir
from concourse.bass_utils import run_bass_kernel_spmd

BUILD_LOG = []

F32 = mybir.dt.float32
BF16 = mybir.dt.bfloat16
U8 = mybir.dt.uint8
AF = mybir.ActivationFunctionType
ALU = mybir.AluOpType
AX = mybir.AxisListType.X

D = 4096
NCF = 32
NT = 1024
NTT = 8
HQ = 32
HKV = 8
LN_EPS = 1e-5
ATT_EPS = 1e-10

CHUNKS = []
for oc in range(HKV):
    CHUNKS.append(("wk", oc))
    CHUNKS.append(("wv", oc))
CHUNKS.append(("apack", 0))
CHUNKS.append(("apack", 1))
CHUNKS.append(("bft", 0))
CHUNKS.append(("bft", 1))
for j in range(HQ):
    CHUNKS.append(("wq", j))
for j in range(32):
    CHUNKS.append(("wo", j))
    CHUNKS.append(("wf", j))
NCHUNK = len(CHUNKS)

N_LAND = 2
N_HR = 4


def build_nc():
    nc = bass.Bass("TRN2", target_bir_lowering=False, debug=False)

    din = {}
    for name, shape in [
        ("x_tok", [NT, D]), ("x_fm", [D, NT]),
        ("wq", [32, 128, 4096]), ("wk", [8, 128, 4096]), ("wv", [8, 128, 4096]),
        ("wo", [32, 128, 4096]), ("wf", [32, 128, 4096]),
        ("apack", [2, 128, 4096]), ("bft", [2, 128, 4096]),
        ("bq", [128, 32]), ("bk", [128, 8]), ("bv", [128, 8]),
        ("bo", [128, 32]), ("bfb", [128, 32]),
        ("creps", [128, 2]), ("sreps", [128, 2]),
        ("ident", [128, 128]),
    ]:
        din[name] = nc.dram_tensor(name, shape, F32, kind="ExternalInput").ap()
    out_fm = nc.dram_tensor("out_fm", [D, NT], F32, kind="ExternalOutput").ap()
    cc_in1 = nc.dram_tensor("cc_in1", [128, 516], F32).ap()
    cc_out1 = nc.dram_tensor("cc_out1", [128, 516], F32).ap()
    cc_in2 = nc.dram_tensor("cc_in2", [128, 516], F32).ap()
    cc_out2 = nc.dram_tensor("cc_out2", [128, 516], F32).ap()

    with ExitStack() as ctx:
        def sbt(name, shape, dt):
            return ctx.enter_context(nc.sbuf_tensor(name, shape, dt))

        def pst(name, shape, dt):
            return ctx.enter_context(nc.psum_tensor(name, shape, dt))

        xn_fm = sbt("xn_fm", [128, NCF * 1024], BF16)
        ring = sbt("ring", [128, N_HR * 2048], BF16)
        land = sbt("land", [128, N_LAND * 2048], F32)
        bft_w = sbt("bft_w", [128, 2 * 4096], BF16)
        kvt_bf = sbt("kvt_bf", [128, 1024], BF16)
        stats = sbt("stats", [128, 32], F32)
        biases = sbt("biases", [128, 118], F32)
        ident_f = sbt("ident_f", [128, 128], F32)
        ident_b = sbt("ident_b", [128, 128], BF16)
        ones_c = sbt("ones_c", [128, 1], BF16)
        q_bf = sbt("q_bf", [128, 2 * 512], BF16)
        xa_bf = sbt("xa_bf", [128, 2 * 1024], BF16)
        e_scr = sbt("e_scr", [128, 2 * 512], F32)
        r_scr = sbt("r_scr", [128, 512], F32)
        g_scr = sbt("g_scr", [128, 2 * 512], F32)
        staging = sbt("staging", [128, 1032], F32)
        x_scr = sbt("x_scr", [128, 2 * 512], F32)
        o_scr = sbt("o_scr", [128, 2 * 512], F32)
        arena = sbt("arena", [128, 65536], U8)

        xt_sl = [arena[:, 0:16384].bitcast(F32), arena[:, 16384:32768].bitcast(F32)]
        xn_tm2 = [arena[:, 32768:40960].bitcast(BF16),
                  arena[:, 49152:57344].bitcast(BF16)]
        k_fm = arena[:, 40960:43008].bitcast(BF16)
        v_fm = arena[:, 43008:45056].bitcast(BF16)
        k_tm = arena[:, 45056:47104].bitcast(BF16)
        v_tm = arena[:, 47104:49152].bitcast(BF16)
        ARENA_P1 = ["xt0", "xt1", "xn_tm0", "xn_tm1", "kfm", "vfm", "ktm",
                    "vtm"]
        attn_bf = arena[:, 0:65536].bitcast(BF16)

        bq_c = biases[:, 0:32]
        bo_c = biases[:, 32:64]
        bfb_c = biases[:, 64:96]
        bk_c = biases[:, 96:104]
        bv_c = biases[:, 104:112]
        creps_c = biases[:, 112:114]
        sreps_c = biases[:, 114:116]
        w128_c = biases[:, 116:118]

        PT = pst("pt", [128, 1024], BF16)
        PT2 = pst("pt2", [128, 1024], BF16)
        PT3 = pst("pt3", [128, 1024], BF16)
        PT4 = pst("pt4", [128, 1024], BF16)
        P0 = pst("p0", [128, 512], F32)
        P1 = pst("p1", [128, 512], F32)
        P2 = pst("p2", [128, 512], F32)
        P3 = pst("p3", [128, 512], F32)
        PS_NAME = {id(PT): "PT", id(PT2): "PT2", id(P0): "P0", id(P1): "P1",
                   id(P2): "P2", id(P3): "P3"}

        Spe = ctx.enter_context(nc.semaphore("spe"))
        Sdv = ctx.enter_context(nc.semaphore("sdv"))
        Sac = ctx.enter_context(nc.semaphore("sac"))
        Spo = ctx.enter_context(nc.semaphore("spo"))
        Scc = ctx.enter_context(nc.semaphore("scc"))
        NQ_Q = {"wq": 8, "xq": 16, "od": 8, "gq": 4}
        QSEM = {
            q: [ctx.enter_context(nc.semaphore(f"s{q}{i}")) for i in range(n)]
            for q, n in NQ_Q.items()
        }
        QIDX = {"wq": 0, "xq": 0, "od": 0, "gq": 0}

        SEMNAME = {id(Spe): "Spe", id(Sdv): "Sdv", id(Sac): "Sac",
                   id(Spo): "Spo", id(Scc): "Scc"}
        for qn, sl in QSEM.items():
            for i, s in enumerate(sl):
                SEMNAME[id(s)] = f"{qn}{i}"
        BUILD_LOG.clear()

        block = ctx.enter_context(nc.Block())

        streams = {"sync": [], "tensor": [], "vector": [], "scalar": [],
                   "gpsimd": []}
        ENG_SEM = {"tensor": Spe, "vector": Sdv, "scalar": Sac,
                   "gpsimd": Spo}
        cnt = {"tensor": 0, "vector": 0, "scalar": 0, "gpsimd": 0, "cc": 0}
        qcnt = {k: [0] * NQ_Q[k] for k in QSEM}
        TRK = {}

        def em(eng, fn):
            streams[eng].append(fn)

        def _add_wait(waits, tok, own_sem):
            if tok is None:
                return
            sem, val = tok
            k = id(sem)
            if k in waits:
                waits[k] = (sem, max(waits[k][1], val))
            else:
                waits[k] = (sem, val)

        def _gather(own_sem, reads, writes):
            waits = {}
            for r in reads:
                e = TRK.get(r)
                if e:
                    _add_wait(waits, e["w"], own_sem)
            for r in writes:
                e = TRK.get(r)
                if e:
                    _add_wait(waits, e["w"], own_sem)
                    for tok in e["rd"].values():
                        _add_wait(waits, tok, own_sem)
            return waits

        def _commit(tok, reads, writes):
            for r in writes:
                TRK[r] = {"w": tok, "rd": {}}
            for r in reads:
                e = TRK.setdefault(r, {"w": None, "rd": {}})
                k = id(tok[0])
                old = e["rd"].get(k)
                if old is None or tok[1] > old[1]:
                    e["rd"][k] = tok

        def step(eng, builder, reads=(), writes=(), extra=(), inc=1):
            own = ENG_SEM[eng]
            waits = _gather(own, reads, writes)
            for tok in extra:
                _add_wait(waits, tok, None)
            wl = list(waits.values())
            cnt[eng] += inc
            tok = (own, cnt[eng])
            BUILD_LOG.append((eng, f"step r={reads} w={writes}",
                              [(SEMNAME[id(s)], v) for s, v in wl],
                              [(SEMNAME[id(own)], cnt[eng])]))
            def fn(e, wl=wl, builder=builder, own=own, inc=inc):
                for s, v in wl:
                    e.wait_ge(s, v)
                inst = builder(e)
                inst.then_inc(own, inc)
            em(eng, fn)
            _commit(tok, reads, writes)
            return tok

        def dma(eng, qkey, out_ap, in_ap, reads=(), writes=()):
            waits = _gather(None, reads, writes)
            wl = list(waits.values())
            r = QIDX[qkey] % NQ_Q[qkey]
            QIDX[qkey] += 1
            qsem = QSEM[qkey][r]
            qcnt[qkey][r] += 16
            tok = (qsem, qcnt[qkey][r])
            BUILD_LOG.append((eng, f"dma r={reads} w={writes}",
                              [(SEMNAME[id(s)], v) for s, v in wl],
                              [(SEMNAME[id(qsem)], qcnt[qkey][r])]))
            def fn(e, wl=wl, out_ap=out_ap, in_ap=in_ap, qsem=qsem):
                for s, v in wl:
                    e.wait_ge(s, v)
                e.dma_start(out=out_ap, in_=in_ap).then_inc(qsem, 16)
            em(eng, fn)
            _commit(tok, reads, writes)
            return tok

        issued_halves = [0]

        def chunk_src(ci):
            kind, idx = CHUNKS[ci]
            return din[kind][idx]

        def issue_half(hi):
            if hi >= NCHUNK * 2:
                return
            assert hi == issued_halves[0], (hi, issued_halves[0])
            ci, part = hi // 2, hi % 2
            src = chunk_src(ci)[:, part * 2048:(part + 1) * 2048]
            dst = land[:, (hi % N_LAND) * 2048:((hi % N_LAND) + 1) * 2048]
            dma("sync", "wq", dst, src, reads=(), writes=(f"land{hi % N_LAND}",))
            issued_halves[0] += 1

        def cast_half(hi):
            ci, part = hi // 2, hi % 2
            kind, idx = CHUNKS[ci]
            lres = f"land{hi % N_LAND}"
            lsl = land[:, (hi % N_LAND) * 2048:((hi % N_LAND) + 1) * 2048]
            if kind == "bft":
                dst = bft_w[:, idx * 4096 + part * 2048: idx * 4096 + (part + 1) * 2048]
                wcol = w128_c[:, idx:idx + 1]
                step("vector", lambda e, dst=dst, lsl=lsl, wcol=wcol:
                     e.tensor_scalar_mul(dst, lsl, wcol),
                     reads=(lres, "biases"), writes=("bft_w",))
            else:
                dres = f"hr{hi % N_HR}"
                dst = ring[:, (hi % N_HR) * 2048:((hi % N_HR) + 1) * 2048]
                if hi % 2 == 0:
                    step("scalar", lambda e, dst=dst, lsl=lsl: e.copy(dst, lsl),
                         reads=(lres,), writes=(dres,))
                else:
                    step("vector", lambda e, dst=dst, lsl=lsl: e.tensor_copy(dst, lsl),
                         reads=(lres,), writes=(dres,))
            issue_half(hi + N_LAND)

        def cast_chunk(ci):
            cast_half(2 * ci)
            cast_half(2 * ci + 1)

        def half_sl(hi):
            return (ring[:, (hi % N_HR) * 2048:((hi % N_HR) + 1) * 2048],
                    f"hr{hi % N_HR}")

        def proj_group(PP, ppres, ci, rhs_fn, extra_mm=None, extra_reads=()):
            for part in range(2):
                wsl, wres = half_sl(2 * ci + part)
                last_part = part == 1 and extra_mm is None
                def bl(e, wsl=wsl, PP=PP, part=part, rhs_fn=rhs_fn,
                       last_part=last_part, extra_mm=extra_mm):
                    last = None
                    for cc in range(16):
                        c = part * 16 + cc
                        last = e.matmul(
                            PP[:], wsl[:, cc * 128:(cc + 1) * 128], rhs_fn(c),
                            start=(c == 0),
                            stop=(c == 31 and last_part))
                    if part == 1 and extra_mm is not None:
                        last = extra_mm(e)
                    return last
                rd = (wres, ) + tuple(extra_reads) if part == 1 else (wres,) + tuple(extra_reads)
                step("tensor", bl, reads=rd, writes=(ppres,))

        def xn_v(c, t0, t1):
            return xn_fm[:, c * 1024 + t0: c * 1024 + t1]

        def attn_v(c, t0, t1):
            return attn_bf[:, c * 1024 + t0: c * 1024 + t1]

        def issue_xtok(i):
            src = din["x_tok"].rearrange("(n p) d -> n p d", p=128)[i]
            dma("scalar", "xq", xt_sl[i % 2], src,
                writes=(f"xt{i % 2}",))

        issue_xtok(0)
        issue_xtok(1)
        dma("scalar", "xq", ident_f[:], din["ident"], writes=("ident_f",))
        bias_toks = []
        for dst, src in [
            (biases[:, 0:32], din["bq"]),
            (biases[:, 32:64], din["bo"]),
            (biases[:, 64:96], din["bfb"]),
            (biases[:, 96:104], din["bk"]),
            (biases[:, 104:112], din["bv"]),
            (biases[:, 112:114], din["creps"]),
            (biases[:, 114:116], din["sreps"]),
        ]:
            bias_toks.append(dma("scalar", "xq", dst, src))
        step("vector", lambda e: e.engine_nop(), extra=tuple(bias_toks),
             writes=("biases",))

        for h in range(N_LAND):
            issue_half(h)

        def init_consts():
            step("vector", lambda e: e.tensor_copy(ident_b[:], ident_f[:]),
                 reads=("ident_f",), writes=("ident_b",))
            step("vector", lambda e: e.tensor_mul(w128_c, creps_c, sreps_c),
                 reads=("biases",), writes=("biases",))
            step("vector", lambda e: e.memset(ones_c[:], 1.0), writes=("ones",))

        def _ln_env(i):
            par = i % 2
            return (par, xt_sl[par], f"xt{par}", xn_tm2[par], f"xn_tm{par}",
                    stats[:, 24:32] if par else stats[:, 0:8], f"st{par}",
                    f"sq{par}")

        def ln_statsA(i):
            par, xt, xres, xn_tm, xnres, st, stres, sqres = _ln_env(i)
            step("vector", lambda e, xt=xt, st=st: e.reduce_sum(
                st[:, 0:1], xt, axis=AX), reads=(xres,), writes=(stres,))
            step("scalar", lambda e, xt=xt, st=st, xn_tm=xn_tm: e.activation(
                xn_tm, xt, AF.Square, accum_out=st[:, 1:2]),
                reads=(xres,), writes=(xnres, sqres))

        def ln_statsB(i):
            par, xt, xres, xn_tm, xnres, st, stres, sqres = _ln_env(i)
            step("vector", lambda e, st=st: e.tensor_scalar_mul(
                st[:, 2:3], st[:, 0:1], 1.0 / D),
                reads=(stres,), writes=(stres,))
            step("vector", lambda e, st=st: e.tensor_mul(
                st[:, 3:4], st[:, 2:3], st[:, 2:3]),
                reads=(stres,), writes=(stres,))
            step("vector", lambda e, st=st: e.scalar_tensor_tensor(
                st[:, 4:5], st[:, 1:2], 1.0 / D, st[:, 3:4],
                ALU.mult, ALU.subtract), reads=(stres, sqres), writes=(stres,))
            step("vector", lambda e, st=st: e.tensor_scalar_add(
                st[:, 4:5], st[:, 4:5], LN_EPS),
                reads=(stres,), writes=(stres,))
            step("scalar", lambda e, st=st: e.activation(
                st[:, 5:6], st[:, 4:5], AF.Ln),
                reads=(stres,), writes=(stres,))
            step("scalar", lambda e, st=st: e.activation(
                st[:, 6:7], st[:, 5:6], AF.Exp, bias=0.0, scale=-0.5),
                reads=(stres,), writes=(stres,))
            step("vector", lambda e, st=st: e.scalar_tensor_tensor(
                st[:, 7:8], st[:, 2:3], -1.0, st[:, 6:7],
                ALU.mult, ALU.mult), reads=(stres,), writes=(stres,))
            step("vector", lambda e, xt=xt, st=st, xn_tm=xn_tm: e.tensor_scalar(
                xn_tm, xt, st[:, 6:7], st[:, 7:8], ALU.mult, ALU.add),
                reads=(xres, stres), writes=(xnres,))
            if i + 2 < NTT:
                issue_xtok(i + 2)

        def ln_transpose(i):
            par = i % 2
            xn_tm = xn_tm2[par]
            xnres = f"xn_tm{par}"
            for b4 in range(4):
                PTx = (PT, PT2, PT3, PT4)[b4]
                pres = ("PT", "PT2", "PT3", "PT4")[b4]
                def trb(e, b4=b4, PTx=PTx, xn_tm=xn_tm):
                    last = None
                    for c8 in range(8):
                        c = b4 * 8 + c8
                        last = e.transpose(
                            PTx[:, c8 * 128:(c8 + 1) * 128],
                            xn_tm[:, c * 128:(c + 1) * 128], ident_b[:])
                    return last
                step("tensor", trb, reads=(xnres, "ident_b"), writes=(pres,))
                c0 = b4 * 8
                if b4 == 3:
                    def ev(e, c0=c0, i=i, PTx=PTx):
                        src = PTx[:].rearrange("p (c t) -> p c t", t=128)
                        dst = xn_fm[:].rearrange("p (c t) -> p c t", t=1024)[
                            :, c0:c0 + 8, i * 128:(i + 1) * 128]
                        return e.tensor_copy(dst, src)
                    step("vector", ev, reads=(pres,), writes=("xn_fm",))
                else:
                    def ev(e, c0=c0, i=i, PTx=PTx):
                        src = PTx[:].rearrange("p (c t) -> p c t", t=128)
                        dst = xn_fm[:].rearrange("p (c t) -> p c t", t=1024)[
                            :, c0:c0 + 8, i * 128:(i + 1) * 128]
                        return e.copy(dst, src)
                    step("scalar", ev, reads=(pres,), writes=("xn_fm",))

        ln_statsA(0)
        init_consts()
        ln_statsA(1)
        for i in range(NTT):
            ln_statsB(i)
            ln_transpose(i)
            if i + 2 < NTT:
                ln_statsA(i + 2)
            if i == 2:
                cast_chunk(0)
            if i == 4:
                cast_chunk(1)

        def kv_epi(s):
            oc, kv, th = s // 4, (s % 4) // 2, s % 2
            PP = P0 if th == 0 else P1
            ppres = "P0" if th == 0 else "P1"
            fmdst, fres = (k_fm, "kfm") if kv == 0 else (v_fm, "vfm")
            bcol = (bk_c if kv == 0 else bv_c)[:, oc:oc + 1]
            if kv == 0:
                esl = e_scr[:, th * 512:(th + 1) * 512]
                rsl = r_scr[:]
                eres, rres = f"e{th}", "r"
                step("vector", lambda e, PP=PP, bcol=bcol, esl=esl:
                     e.tensor_scalar(esl, PP[:], bcol, 0.0, ALU.add, ALU.min),
                     reads=(ppres, "biases"), writes=(eres,))
                step("scalar", lambda e, esl=esl: e.activation(
                    esl, esl, AF.Exp), reads=(eres,), writes=(eres,))
                step("vector", lambda e, PP=PP, bcol=bcol, rsl=rsl:
                     e.tensor_scalar(rsl, PP[:], bcol, 0.0, ALU.add, ALU.max),
                     reads=(ppres, "biases"), writes=(rres,))
                step("vector", lambda e, fmdst=fmdst, th=th, esl=esl, rsl=rsl:
                     e.tensor_add(fmdst[:, th * 512:(th + 1) * 512], esl, rsl),
                     reads=(eres, rres), writes=(fres,))
            else:
                step("vector", lambda e, PP=PP, bcol=bcol, fmdst=fmdst, th=th:
                     e.tensor_scalar_add(
                         fmdst[:, th * 512:(th + 1) * 512], PP[:], bcol),
                     reads=(ppres, "biases"), writes=(fres,))

        def kv_tail_tr(oc):
            for kv, fmdst, fres in ((0, k_fm, "kfm"), (1, v_fm, "vfm")):
                PTx = PT if kv == 0 else PT2
                pres = "PT" if kv == 0 else "PT2"
                def trk(e, fmdst=fmdst, PTx=PTx):
                    last = None
                    for b in range(8):
                        last = e.transpose(
                            PTx[:, b * 128:(b + 1) * 128],
                            fmdst[:, b * 128:(b + 1) * 128], ident_b[:])
                    return last
                step("tensor", trk, reads=(fres, "ident_b"), writes=(pres,))
                tmdst, tres = (k_tm, "ktm") if kv == 0 else (v_tm, "vtm")
                step("vector", lambda e, tmdst=tmdst, PTx=PTx: e.tensor_copy(
                    tmdst[:], PTx[:]), reads=(pres,), writes=(tres,))

        def kv_tail_mm(oc):
            PK = P2 if oc % 2 == 0 else P3
            pkres = "P2" if oc % 2 == 0 else "P3"
            def kvtmm(e, PK=PK):
                last = None
                for b in range(8):
                    last = e.matmul(
                        PK[:, 0:128], k_tm[:, b * 128:(b + 1) * 128],
                        v_tm[:, b * 128:(b + 1) * 128],
                        start=(b == 0), stop=(b == 7))
                return last
            step("tensor", kvtmm, reads=("ktm", "vtm"), writes=(pkres,))
            def normm(e, PK=PK):
                last = None
                for b in range(8):
                    last = e.matmul(
                        PK[:, 128:129], k_tm[:, b * 128:(b + 1) * 128],
                        ones_c[:], start=(b == 0), stop=(b == 7))
                return last
            step("tensor", normm, reads=("ktm", "ones"), writes=(pkres,))
            def kvev(e, oc=oc, PK=PK):
                return e.tensor_copy(
                    staging[:, oc * 129:oc * 129 + 129], PK[:, 0:129])
            step("vector", kvev, reads=(pkres,),
                 writes=("stagA" if oc < 4 else "stagB",))

        def emit_collective(half):
            sres = "stagA" if half == 0 else "stagB"
            ccin = cc_in1 if half == 0 else cc_in2
            ccout = cc_out1 if half == 0 else cc_out2
            inres, outres = f"ccin{half}", f"ccout{half}"
            sl = staging[:, half * 516:(half + 1) * 516]
            dma("scalar", "xq", ccin, sl, reads=(sres,), writes=(inres,))
            cc_waits = list(_gather(None, (inres,), (outres,)).values())
            cnt["cc"] += 1
            cc_tok = (Scc, cnt["cc"])
            BUILD_LOG.append(("gpsimd", f"collective{half}",
                              [(SEMNAME[id(s)], v) for s, v in cc_waits],
                              [("Scc", cnt["cc"])]))
            ccv = cnt["cc"]
            def fn_cc(e, wl=cc_waits, ccin=ccin, ccout=ccout, ccv=ccv):
                for s, v in wl:
                    e.wait_ge(s, v)
                e.collective_compute(
                    "AllReduce", ALU.add,
                    replica_groups=[[0, 1, 2, 3], [4, 5, 6, 7]],
                    ins=[ccin], outs=[ccout],
                ).then_inc(Scc, 1)
            em("gpsimd", fn_cc)
            _commit(cc_tok, (inres,), (outres,))
            dma("gpsimd", "gq", sl, ccout, reads=(outres,), writes=(sres,))

        for s in range(32):
            oc, kv, th = s // 4, (s % 4) // 2, s % 2
            ci = 2 * oc + kv
            proj_group(P0 if th == 0 else P1, "P0" if th == 0 else "P1", ci,
                       lambda c, th=th: xn_v(c, th * 512, (th + 1) * 512),
                       extra_reads=("xn_fm",))
            if s % 4 == 2:
                cast_chunk(2 * oc + 2)
            if s % 4 == 3:
                cast_chunk(2 * oc + 3)
            if s >= 1:
                kv_epi(s - 1)
            if s % 4 == 0 and oc >= 1:
                kv_tail_tr(oc - 1)
            if s % 4 == 1 and oc >= 1:
                kv_tail_mm(oc - 1)
                if oc - 1 == 3:
                    emit_collective(0)
        kv_epi(31)
        kv_tail_tr(7)
        kv_tail_mm(7)
        emit_collective(1)

        xa_epis = []
        for s in range(4):
            gi, th = s // 2, s % 2
            PP = P0 if th == 0 else P1
            ppres = "P0" if th == 0 else "P1"
            proj_group(PP, ppres, 16 + gi,
                       lambda c, th=th: xn_v(c, th * 512, (th + 1) * 512),
                       extra_reads=("xn_fm",))
            if s == 1:
                cast_chunk(18)
            if s == 3:
                cast_chunk(19)
            if s >= 1:
                xa_epis[s - 1]()
            def mkepi(PP=PP, ppres=ppres, gi=gi, th=th):
                def f():
                    step("vector", lambda e, PP=PP, gi=gi, th=th: e.tensor_copy(
                        xa_bf[:, gi * 1024 + th * 512:
                              gi * 1024 + (th + 1) * 512],
                        PP[:]), reads=(ppres,), writes=("xa",))
                return f
            xa_epis.append(mkepi())

        xa_epis[3]()
        stag_v = staging[:].rearrange("p (h c) -> p h c", c=129)
        kvt_v = kvt_bf[:].rearrange("p (h c) -> p h c", c=128)

        def emit_cc_consumers():
            step("vector", lambda e: e.tensor_copy(kvt_v, stag_v[:, :, 0:128]),
                 reads=("stagA", "stagB"), writes=("kvt",))
            step("vector", lambda e: e.tensor_scalar_add(
                stats[:, 8:16].rearrange("p (h o) -> p h o", o=1),
                stag_v[:, :, 128:129], ATT_EPS),
                reads=("stagA", "stagB"), writes=("stCC",))
            step("vector", lambda e: e.reciprocal(
                stats[:, 16:24], stats[:, 8:16]),
                reads=("stCC",), writes=("stCC",),
                extra=((Sdv, cnt["vector"]),))

        cast_chunk(20)
        first_attn_state = [True]

        def q_phi(k):
            j, th = k // 2, k % 2
            PP = P0 if th == 0 else P1
            ppres = "P0" if th == 0 else "P1"
            bcol = bq_c[:, j:j + 1]
            esl = e_scr[:, th * 512:(th + 1) * 512]
            rsl = r_scr[:]
            eres, rres = f"e{th}", "r"
            qsl = q_bf[:, th * 512:(th + 1) * 512]
            qres = f"q{th}"
            step("vector", lambda e, PP=PP, bcol=bcol, esl=esl:
                 e.tensor_scalar(esl, PP[:], bcol, 0.0, ALU.add, ALU.min),
                 reads=(ppres, "biases"), writes=(eres,))
            step("scalar", lambda e, esl=esl: e.activation(
                esl, esl, AF.Exp), reads=(eres,), writes=(eres,))
            step("vector", lambda e, PP=PP, bcol=bcol, rsl=rsl:
                 e.tensor_scalar(rsl, PP[:], bcol, 0.0, ALU.add, ALU.max),
                 reads=(ppres, "biases"), writes=(rres,))
            step("vector", lambda e, qsl=qsl, esl=esl, rsl=rsl:
                 e.tensor_add(qsl, esl, rsl), reads=(eres, rres), writes=(qres,))

        def attn_mm(k):
            j, th = k // 2, k % 2
            PA = P2 if th == 0 else P3
            pares = "P2" if th == 0 else "P3"
            qsl = q_bf[:, th * 512:(th + 1) * 512]
            gcol = j // 4
            step("tensor", lambda e, PA=PA, qsl=qsl, gcol=gcol: e.matmul(
                PA[:], kvt_bf[:, gcol * 128:(gcol + 1) * 128], qsl,
                start=True, stop=True),
                reads=("kvt", f"q{th}"), writes=(pares,))

        def attn_evac(k):
            j, th = k // 2, k % 2
            PA = P2 if th == 0 else P3
            pares = "P2" if th == 0 else "P3"
            gcol = j // 4
            aw = ("attn",) + tuple(ARENA_P1) if first_attn_state[0] else ("attn",)
            first_attn_state[0] = False
            step("vector", lambda e, PA=PA, j=j, th=th, gcol=gcol:
                 e.tensor_scalar_mul(
                     attn_v(j, th * 512, (th + 1) * 512), PA[:],
                     stats[:, 16 + gcol:17 + gcol]),
                 reads=(pares, "stCC"), writes=aw)

        for k in range(64):
            j, th = k // 2, k % 2
            cq = 20 + j
            proj_group(P0 if th == 0 else P1, "P0" if th == 0 else "P1", cq,
                       lambda c, th=th: xn_v(c, th * 512, (th + 1) * 512),
                       extra_reads=("xn_fm",))
            if j + 1 < HQ:
                cast_half(2 * (cq + 1) + th)
            if k == 1:
                emit_cc_consumers()
            if k >= 1:
                attn_mm(k - 1)
                q_phi(k)
            else:
                q_phi(0)
            if k >= 2:
                attn_evac(k - 2)
        attn_mm(63)
        attn_evac(62)
        attn_evac(63)

        xfm_r = din["x_fm"].rearrange("(r p) t -> r p t", p=128)
        out_r = out_fm.rearrange("(r p) t -> r p t", p=128)

        def issue_xfm(k):
            j, th = k // 2, k % 2
            src = xfm_r[j][:, th * 512:(th + 1) * 512]
            dst = x_scr[:, (k % 2) * 512:((k % 2) + 1) * 512]
            dma("scalar", "xq", dst, src, writes=(f"x{k % 2}",))

        for k in range(2):
            issue_xfm(k)

        def oc_epi(k):
            j, th = k // 2, k % 2
            PS = P0 if th == 0 else P1
            psres = "P0" if th == 0 else "P1"
            PG = P2 if th == 0 else P3
            pgres = "P2" if th == 0 else "P3"
            gsl = g_scr[:, th * 512:(th + 1) * 512]
            gres = f"g{th}"
            step("scalar", lambda e, PG=PG, j=j, gsl=gsl: e.activation(
                gsl, PG[:], AF.Silu, bias=bfb_c[:, j:j + 1], scale=1.0),
                reads=(pgres, "biases"), writes=(gres,))
            step("vector", lambda e, PS=PS, j=j, gsl=gsl:
                 e.scalar_tensor_tensor(gsl, PS[:], bo_c[:, j:j + 1], gsl,
                                        ALU.add, ALU.mult),
                 reads=(psres, "biases", gres), writes=(gres,))
            osl = o_scr[:, (k % 2) * 512:((k % 2) + 1) * 512]
            ores = f"o{k % 2}"
            step("vector", lambda e, osl=osl, gsl=gsl, k=k: e.tensor_add(
                osl, gsl, x_scr[:, (k % 2) * 512:((k % 2) + 1) * 512]),
                reads=(gres, f"x{k % 2}"), writes=(ores,))
            dma("scalar", "od", out_r[j][:, th * 512:(th + 1) * 512],
                osl, reads=(ores,))
            if k + 2 < 64:
                issue_xfm(k + 2)

        cast_chunk(52)
        cast_chunk(53)
        for k in range(64):
            j, th = k // 2, k % 2
            co, cf = 52 + 2 * j, 53 + 2 * j
            PS = P0 if th == 0 else P1
            psres = "P0" if th == 0 else "P1"
            PG = P2 if th == 0 else P3
            pgres = "P2" if th == 0 else "P3"
            proj_group(PS, psres, co,
                       lambda c, th=th: attn_v(c, th * 512, (th + 1) * 512),
                       extra_reads=("attn",))
            def upd(e, PG=PG, th=th, j=j):
                last = None
                for gi in range(2):
                    last = e.matmul(
                        PG[:], bft_w[:, gi * 4096 + j * 128:
                                     gi * 4096 + (j + 1) * 128],
                        xa_bf[:, gi * 1024 + th * 512:
                              gi * 1024 + (th + 1) * 512],
                        start=False, stop=(gi == 1))
                return last
            proj_group(PG, pgres, cf,
                       lambda c, th=th: xn_v(c, th * 512, (th + 1) * 512),
                       extra_mm=upd, extra_reads=("xn_fm", "bft_w", "xa"))
            if k >= 1:
                oc_epi(k - 1)
            if th == 1 and j + 1 < 32:
                cast_chunk(co + 2)
                cast_chunk(cf + 2)
        oc_epi(63)

        od_final = [(QSEM["od"][i], qcnt["od"][i]) for i in range(NQ_Q["od"])
                    if qcnt["od"][i] > 0]
        BUILD_LOG.append(("scalar", "final",
                          [(SEMNAME[id(s)], v) for s, v in od_final], []))
        def fin(e, fl=od_final):
            for s, v in fl:
                e.wait_ge(s, v)
        em("scalar", fin)

        @block.sync
        def _(eng):
            for fn in streams["sync"]:
                fn(eng)

        @block.tensor
        def _(eng):
            for fn in streams["tensor"]:
                fn(eng)

        @block.vector
        def _(eng):
            for fn in streams["vector"]:
                fn(eng)

        @block.scalar
        def _(eng):
            for fn in streams["scalar"]:
                fn(eng)

        @block.gpsimd
        def _(eng):
            for fn in streams["gpsimd"]:
                fn(eng)

    return nc


def _pack_w(w):
    O, F = w.shape
    return np.ascontiguousarray(
        w.reshape(O // 128, 128, F // 128, 128).transpose(0, 3, 2, 1)
        .reshape(O // 128, 128, F))


def kernel(L0_attn_hidden_states, L0_attn_Wq, L0_attn_bq, L0_attn_Wk,
           L0_attn_bk, L0_attn_Wv, L0_attn_bv, L0_attn_Wo, L0_attn_bo,
           L0_ffn_W, L0_ffn_bias, L0_ffn_A, L0_ffn_Bf, L0_ffn_coeffs,
           L0_ffn_scales):
    x = np.ascontiguousarray(np.asarray(L0_attn_hidden_states, dtype=np.float32))
    wq = _pack_w(np.asarray(L0_attn_Wq, np.float32))
    wk = _pack_w(np.asarray(L0_attn_Wk, np.float32))
    wv = _pack_w(np.asarray(L0_attn_Wv, np.float32))
    wo = _pack_w(np.asarray(L0_attn_Wo, np.float32))
    wf = _pack_w(np.asarray(L0_ffn_W, np.float32))
    A = np.asarray(L0_ffn_A, np.float32)
    Bf = np.asarray(L0_ffn_Bf, np.float32)
    apack = np.ascontiguousarray(
        A.reshape(2, 8, 16, 32, 128)
        .transpose(0, 4, 3, 1, 2)
        .reshape(2, 128, 4096))
    bft = np.ascontiguousarray(
        Bf.transpose(0, 2, 1).reshape(2, 128, 4096))

    def bias_cols(b, n):
        return np.ascontiguousarray(np.asarray(b, np.float32).reshape(n, 128).T)

    bq = bias_cols(L0_attn_bq, 32)
    bk = bias_cols(L0_attn_bk, 8)
    bv = bias_cols(L0_attn_bv, 8)
    bo = bias_cols(L0_attn_bo, 32)
    bfb = bias_cols(L0_ffn_bias, 32)
    coeffs = np.asarray(L0_ffn_coeffs, np.float32)
    scales = np.asarray(L0_ffn_scales, np.float32)
    creps = np.ascontiguousarray(
        np.repeat(coeffs.reshape(2, 8), 16, axis=1).reshape(2, 128).T)
    sreps = np.ascontiguousarray(
        np.repeat(scales.reshape(2, 8), 16, axis=1).reshape(2, 128).T)
    ident = np.eye(128, dtype=np.float32)

    x_tok_all = x.reshape(2, 4, 1024, 4096)
    x_fm_all = np.ascontiguousarray(x.transpose(2, 0, 1).reshape(4096, 8192))

    shared = {
        "wq": wq, "wk": wk, "wv": wv, "wo": wo, "wf": wf,
        "apack": apack, "bft": bft, "bq": bq, "bk": bk, "bv": bv,
        "bo": bo, "bfb": bfb, "creps": creps, "sreps": sreps, "ident": ident,
    }
    in_maps = []
    for core in range(8):
        b, c = core // 4, core % 4
        m = dict(shared)
        m["x_tok"] = np.ascontiguousarray(x_tok_all[b, c])
        m["x_fm"] = np.ascontiguousarray(
            x_fm_all[:, (b * 4 + c) * 1024:(b * 4 + c + 1) * 1024])
        in_maps.append(m)

    nc = build_nc()
    res = run_bass_kernel_spmd(nc, in_maps, core_ids=list(range(8)))
    out_fm_full = np.concatenate(
        [res.results[core]["out_fm"] for core in range(8)], axis=1)
    out = out_fm_full.reshape(4096, 2, 4096).transpose(1, 2, 0)
    return np.ascontiguousarray(out)


# version 15
# speedup vs baseline: 1.0247x; 1.0247x over previous
import numpy as np
from contextlib import ExitStack

import concourse.bass as bass
import concourse.mybir as mybir
from concourse.bass_utils import run_bass_kernel_spmd

BUILD_LOG = []

F32 = mybir.dt.float32
BF16 = mybir.dt.bfloat16
U8 = mybir.dt.uint8
AF = mybir.ActivationFunctionType
ALU = mybir.AluOpType
AX = mybir.AxisListType.X

D = 4096
NCF = 32
NT = 1024
NTT = 8
HQ = 32
HKV = 8
LN_EPS = 1e-5
ATT_EPS = 1e-10

CHUNKS = []
for oc in range(HKV):
    CHUNKS.append(("wk", oc))
    CHUNKS.append(("wv", oc))
CHUNKS.append(("apack", 0))
CHUNKS.append(("apack", 1))
CHUNKS.append(("bft", 0))
CHUNKS.append(("bft", 1))
for j in range(HQ):
    CHUNKS.append(("wq", j))
for j in range(32):
    CHUNKS.append(("wo", j))
    CHUNKS.append(("wf", j))
NCHUNK = len(CHUNKS)

N_LAND = 2
N_HR = 4


def build_nc():
    nc = bass.Bass("TRN2", target_bir_lowering=False, debug=False)

    din = {}
    for name, shape in [
        ("x_tok", [NT, D]), ("x_fm", [D, NT]),
        ("wq", [32, 128, 4096]), ("wk", [8, 128, 4096]), ("wv", [8, 128, 4096]),
        ("wo", [32, 128, 4096]), ("wf", [32, 128, 4096]),
        ("apack", [2, 128, 4096]), ("bft", [2, 128, 4096]),
        ("bq", [128, 32]), ("bk", [128, 8]), ("bv", [128, 8]),
        ("bo", [128, 32]), ("bfb", [128, 32]),
        ("creps", [128, 2]), ("sreps", [128, 2]),
        ("ident", [128, 128]),
    ]:
        din[name] = nc.dram_tensor(name, shape, F32, kind="ExternalInput").ap()
    out_fm = nc.dram_tensor("out_fm", [D, NT], F32, kind="ExternalOutput").ap()
    cc_in1 = nc.dram_tensor("cc_in1", [128, 516], F32).ap()
    cc_out1 = nc.dram_tensor("cc_out1", [128, 516], F32).ap()
    cc_in2 = nc.dram_tensor("cc_in2", [128, 516], F32).ap()
    cc_out2 = nc.dram_tensor("cc_out2", [128, 516], F32).ap()

    with ExitStack() as ctx:
        def sbt(name, shape, dt):
            return ctx.enter_context(nc.sbuf_tensor(name, shape, dt))

        def pst(name, shape, dt):
            return ctx.enter_context(nc.psum_tensor(name, shape, dt))

        xn_fm = sbt("xn_fm", [128, NCF * 1024], BF16)
        ring = sbt("ring", [128, N_HR * 2048], BF16)
        land = sbt("land", [128, N_LAND * 2048], F32)
        bft_w = sbt("bft_w", [128, 2 * 4096], BF16)
        kvt_bf = sbt("kvt_bf", [128, 1024], BF16)
        stats = sbt("stats", [128, 32], F32)
        biases = sbt("biases", [128, 118], F32)
        ident_f = sbt("ident_f", [128, 128], F32)
        ident_b = sbt("ident_b", [128, 128], BF16)
        ones_c = sbt("ones_c", [128, 1], BF16)
        q_bf = sbt("q_bf", [128, 2 * 512], BF16)
        xa_bf = sbt("xa_bf", [128, 2 * 1024], BF16)
        e_scr = sbt("e_scr", [128, 2 * 512], F32)
        r_scr = sbt("r_scr", [128, 512], F32)
        g_scr = sbt("g_scr", [128, 2 * 512], F32)
        staging = sbt("staging", [128, 1032], F32)
        x_scr = sbt("x_scr", [128, 2 * 512], F32)
        o_scr = sbt("o_scr", [128, 2 * 512], F32)
        arena = sbt("arena", [128, 65536], U8)

        xt_sl = [arena[:, 0:16384].bitcast(F32), arena[:, 16384:32768].bitcast(F32)]
        xn_tm2 = [arena[:, 32768:40960].bitcast(BF16),
                  arena[:, 49152:57344].bitcast(BF16)]
        k_fm = arena[:, 40960:43008].bitcast(BF16)
        v_fm = arena[:, 43008:45056].bitcast(BF16)
        k_tm = arena[:, 45056:47104].bitcast(BF16)
        v_tm = arena[:, 47104:49152].bitcast(BF16)
        ARENA_P1 = ["xt0", "xt1", "xn_tm0", "xn_tm1", "kfm", "vfm", "ktm",
                    "vtm"]
        attn_bf = arena[:, 0:65536].bitcast(BF16)

        bq_c = biases[:, 0:32]
        bo_c = biases[:, 32:64]
        bfb_c = biases[:, 64:96]
        bk_c = biases[:, 96:104]
        bv_c = biases[:, 104:112]
        creps_c = biases[:, 112:114]
        sreps_c = biases[:, 114:116]
        w128_c = biases[:, 116:118]

        PT = pst("pt", [128, 1024], BF16)
        PT2 = pst("pt2", [128, 1024], BF16)
        PT3 = pst("pt3", [128, 1024], BF16)
        PT4 = pst("pt4", [128, 1024], BF16)
        P0 = pst("p0", [128, 512], F32)
        P1 = pst("p1", [128, 512], F32)
        P2 = pst("p2", [128, 512], F32)
        P3 = pst("p3", [128, 512], F32)
        PS_NAME = {id(PT): "PT", id(PT2): "PT2", id(P0): "P0", id(P1): "P1",
                   id(P2): "P2", id(P3): "P3"}

        Spe = ctx.enter_context(nc.semaphore("spe"))
        Sdv = ctx.enter_context(nc.semaphore("sdv"))
        Sac = ctx.enter_context(nc.semaphore("sac"))
        Spo = ctx.enter_context(nc.semaphore("spo"))
        Scc = ctx.enter_context(nc.semaphore("scc"))
        NQ_Q = {"wq": 16, "xq": 16, "od": 8, "gq": 4}
        QSEM = {
            q: [ctx.enter_context(nc.semaphore(f"s{q}{i}")) for i in range(n)]
            for q, n in NQ_Q.items()
        }
        QIDX = {"wq": 0, "xq": 0, "od": 0, "gq": 0}

        SEMNAME = {id(Spe): "Spe", id(Sdv): "Sdv", id(Sac): "Sac",
                   id(Spo): "Spo", id(Scc): "Scc"}
        for qn, sl in QSEM.items():
            for i, s in enumerate(sl):
                SEMNAME[id(s)] = f"{qn}{i}"
        BUILD_LOG.clear()

        block = ctx.enter_context(nc.Block())

        streams = {"sync": [], "tensor": [], "vector": [], "scalar": [],
                   "gpsimd": []}
        ENG_SEM = {"tensor": Spe, "vector": Sdv, "scalar": Sac,
                   "gpsimd": Spo}
        cnt = {"tensor": 0, "vector": 0, "scalar": 0, "gpsimd": 0, "cc": 0}
        qcnt = {k: [0] * NQ_Q[k] for k in QSEM}
        TRK = {}

        def em(eng, fn):
            streams[eng].append(fn)

        def _add_wait(waits, tok, own_sem):
            if tok is None:
                return
            sem, val = tok
            k = id(sem)
            if k in waits:
                waits[k] = (sem, max(waits[k][1], val))
            else:
                waits[k] = (sem, val)

        def _gather(own_sem, reads, writes):
            waits = {}
            for r in reads:
                e = TRK.get(r)
                if e:
                    _add_wait(waits, e["w"], own_sem)
            for r in writes:
                e = TRK.get(r)
                if e:
                    _add_wait(waits, e["w"], own_sem)
                    for tok in e["rd"].values():
                        _add_wait(waits, tok, own_sem)
            return waits

        def _commit(tok, reads, writes):
            for r in writes:
                TRK[r] = {"w": tok, "rd": {}}
            for r in reads:
                e = TRK.setdefault(r, {"w": None, "rd": {}})
                k = id(tok[0])
                old = e["rd"].get(k)
                if old is None or tok[1] > old[1]:
                    e["rd"][k] = tok

        def step(eng, builder, reads=(), writes=(), extra=(), inc=1):
            own = ENG_SEM[eng]
            waits = _gather(own, reads, writes)
            for tok in extra:
                _add_wait(waits, tok, None)
            wl = list(waits.values())
            cnt[eng] += inc
            tok = (own, cnt[eng])
            BUILD_LOG.append((eng, f"step r={reads} w={writes}",
                              [(SEMNAME[id(s)], v) for s, v in wl],
                              [(SEMNAME[id(own)], cnt[eng])]))
            def fn(e, wl=wl, builder=builder, own=own, inc=inc):
                for s, v in wl:
                    e.wait_ge(s, v)
                inst = builder(e)
                inst.then_inc(own, inc)
            em(eng, fn)
            _commit(tok, reads, writes)
            return tok

        def dma(eng, qkey, out_ap, in_ap, reads=(), writes=()):
            waits = _gather(None, reads, writes)
            wl = list(waits.values())
            r = QIDX[qkey] % NQ_Q[qkey]
            QIDX[qkey] += 1
            qsem = QSEM[qkey][r]
            qcnt[qkey][r] += 16
            tok = (qsem, qcnt[qkey][r])
            BUILD_LOG.append((eng, f"dma r={reads} w={writes}",
                              [(SEMNAME[id(s)], v) for s, v in wl],
                              [(SEMNAME[id(qsem)], qcnt[qkey][r])]))
            def fn(e, wl=wl, out_ap=out_ap, in_ap=in_ap, qsem=qsem):
                for s, v in wl:
                    e.wait_ge(s, v)
                e.dma_start(out=out_ap, in_=in_ap).then_inc(qsem, 16)
            em(eng, fn)
            _commit(tok, reads, writes)
            return tok

        issued_halves = [0]

        def chunk_src(ci):
            kind, idx = CHUNKS[ci]
            return din[kind][idx]

        def issue_half(hi):
            if hi >= NCHUNK * 2:
                return
            assert hi == issued_halves[0], (hi, issued_halves[0])
            ci, part = hi // 2, hi % 2
            src = chunk_src(ci)[:, part * 2048:(part + 1) * 2048]
            dst = land[:, (hi % N_LAND) * 2048:((hi % N_LAND) + 1) * 2048]
            dma("sync", "wq", dst, src, reads=(), writes=(f"land{hi % N_LAND}",))
            issued_halves[0] += 1

        def cast_half(hi):
            ci, part = hi // 2, hi % 2
            kind, idx = CHUNKS[ci]
            lres = f"land{hi % N_LAND}"
            lsl = land[:, (hi % N_LAND) * 2048:((hi % N_LAND) + 1) * 2048]
            if kind == "bft":
                dst = bft_w[:, idx * 4096 + part * 2048: idx * 4096 + (part + 1) * 2048]
                wcol = w128_c[:, idx:idx + 1]
                step("vector", lambda e, dst=dst, lsl=lsl, wcol=wcol:
                     e.tensor_scalar_mul(dst, lsl, wcol),
                     reads=(lres, "biases"), writes=("bft_w",))
            else:
                dres = f"hr{hi % N_HR}"
                dst = ring[:, (hi % N_HR) * 2048:((hi % N_HR) + 1) * 2048]
                if hi % 2 == 0:
                    step("scalar", lambda e, dst=dst, lsl=lsl: e.copy(dst, lsl),
                         reads=(lres,), writes=(dres,))
                else:
                    step("vector", lambda e, dst=dst, lsl=lsl: e.tensor_copy(dst, lsl),
                         reads=(lres,), writes=(dres,))
            issue_half(hi + N_LAND)

        def cast_chunk(ci):
            cast_half(2 * ci)
            cast_half(2 * ci + 1)

        def half_sl(hi):
            return (ring[:, (hi % N_HR) * 2048:((hi % N_HR) + 1) * 2048],
                    f"hr{hi % N_HR}")

        def proj_group(PP, ppres, ci, rhs_fn, extra_mm=None, extra_reads=()):
            for part in range(2):
                wsl, wres = half_sl(2 * ci + part)
                last_part = part == 1 and extra_mm is None
                def bl(e, wsl=wsl, PP=PP, part=part, rhs_fn=rhs_fn,
                       last_part=last_part, extra_mm=extra_mm):
                    last = None
                    for cc in range(16):
                        c = part * 16 + cc
                        last = e.matmul(
                            PP[:], wsl[:, cc * 128:(cc + 1) * 128], rhs_fn(c),
                            start=(c == 0),
                            stop=(c == 31 and last_part))
                    if part == 1 and extra_mm is not None:
                        last = extra_mm(e)
                    return last
                rd = (wres, ) + tuple(extra_reads) if part == 1 else (wres,) + tuple(extra_reads)
                step("tensor", bl, reads=rd, writes=(ppres,))

        def xn_v(c, t0, t1):
            return xn_fm[:, c * 1024 + t0: c * 1024 + t1]

        def attn_v(c, t0, t1):
            return attn_bf[:, c * 1024 + t0: c * 1024 + t1]

        def issue_xtok(i):
            src = din["x_tok"].rearrange("(n p) d -> n p d", p=128)[i]
            dma("sync", "wq", xt_sl[i % 2], src,
                writes=(f"xt{i % 2}",))

        issue_xtok(0)
        issue_xtok(1)
        dma("scalar", "xq", ident_f[:], din["ident"], writes=("ident_f",))
        bias_toks = []
        for dst, src in [
            (biases[:, 0:32], din["bq"]),
            (biases[:, 32:64], din["bo"]),
            (biases[:, 64:96], din["bfb"]),
            (biases[:, 96:104], din["bk"]),
            (biases[:, 104:112], din["bv"]),
            (biases[:, 112:114], din["creps"]),
            (biases[:, 114:116], din["sreps"]),
        ]:
            bias_toks.append(dma("scalar", "xq", dst, src))
        step("vector", lambda e: e.engine_nop(), extra=tuple(bias_toks),
             writes=("biases",))

        for h in range(N_LAND):
            issue_half(h)

        def init_consts():
            step("vector", lambda e: e.tensor_copy(ident_b[:], ident_f[:]),
                 reads=("ident_f",), writes=("ident_b",))
            step("vector", lambda e: e.tensor_mul(w128_c, creps_c, sreps_c),
                 reads=("biases",), writes=("biases",))
            step("vector", lambda e: e.memset(ones_c[:], 1.0), writes=("ones",))

        def ln_stats(i):
            par = i % 2
            xt = xt_sl[par]
            xres = f"xt{par}"
            xn_tm = xn_tm2[par]
            xnres = f"xn_tm{par}"
            st = stats[:, 24:32] if par else stats[:, 0:8]
            stres = f"st{par}"
            sqres = f"sq{par}"
            step("vector", lambda e, xt=xt, st=st: e.reduce_sum(
                st[:, 0:1], xt, axis=AX), reads=(xres,), writes=(stres,))
            step("scalar", lambda e, xt=xt, st=st, xn_tm=xn_tm: e.activation(
                xn_tm, xt, AF.Square, accum_out=st[:, 1:2]),
                reads=(xres,), writes=(xnres, sqres))
            step("vector", lambda e, st=st: e.tensor_scalar_mul(
                st[:, 2:3], st[:, 0:1], 1.0 / D),
                reads=(stres,), writes=(stres,))
            step("vector", lambda e, st=st: e.tensor_mul(
                st[:, 3:4], st[:, 2:3], st[:, 2:3]),
                reads=(stres,), writes=(stres,))
            step("vector", lambda e, st=st: e.scalar_tensor_tensor(
                st[:, 4:5], st[:, 1:2], 1.0 / D, st[:, 3:4],
                ALU.mult, ALU.subtract), reads=(stres, sqres), writes=(stres,))
            step("vector", lambda e, st=st: e.tensor_scalar_add(
                st[:, 4:5], st[:, 4:5], LN_EPS),
                reads=(stres,), writes=(stres,))
            step("scalar", lambda e, st=st: e.activation(
                st[:, 5:6], st[:, 4:5], AF.Ln),
                reads=(stres,), writes=(stres,))
            step("scalar", lambda e, st=st: e.activation(
                st[:, 6:7], st[:, 5:6], AF.Exp, bias=0.0, scale=-0.5),
                reads=(stres,), writes=(stres,))
            step("vector", lambda e, st=st: e.scalar_tensor_tensor(
                st[:, 7:8], st[:, 2:3], -1.0, st[:, 6:7],
                ALU.mult, ALU.mult), reads=(stres,), writes=(stres,))
            step("vector", lambda e, xt=xt, st=st, xn_tm=xn_tm: e.tensor_scalar(
                xn_tm, xt, st[:, 6:7], st[:, 7:8], ALU.mult, ALU.add),
                reads=(xres, stres), writes=(xnres,))
            if i + 2 < NTT:
                issue_xtok(i + 2)

        def ln_transpose(i):
            par = i % 2
            xn_tm = xn_tm2[par]
            xnres = f"xn_tm{par}"
            for b4 in range(4):
                PTx = (PT, PT2, PT3, PT4)[b4]
                pres = ("PT", "PT2", "PT3", "PT4")[b4]
                def trb(e, b4=b4, PTx=PTx, xn_tm=xn_tm):
                    last = None
                    for c8 in range(8):
                        c = b4 * 8 + c8
                        last = e.transpose(
                            PTx[:, c8 * 128:(c8 + 1) * 128],
                            xn_tm[:, c * 128:(c + 1) * 128], ident_b[:])
                    return last
                step("tensor", trb, reads=(xnres, "ident_b"), writes=(pres,))
                c0 = b4 * 8
                def ev(e, c0=c0, i=i, PTx=PTx):
                    src = PTx[:].rearrange("p (c t) -> p c t", t=128)
                    dst = xn_fm[:].rearrange("p (c t) -> p c t", t=1024)[
                        :, c0:c0 + 8, i * 128:(i + 1) * 128]
                    return e.copy(dst, src)
                step("scalar", ev, reads=(pres,), writes=("xn_fm",))

        ln_stats(0)
        init_consts()
        for i in range(NTT):
            if i + 1 < NTT:
                ln_stats(i + 1)
            ln_transpose(i)
            if i == 4:
                cast_chunk(0)
                cast_chunk(1)

        def kv_epi(s):
            oc, kv, th = s // 4, (s % 4) // 2, s % 2
            PP = P0 if th == 0 else P1
            ppres = "P0" if th == 0 else "P1"
            fmdst, fres = (k_fm, "kfm") if kv == 0 else (v_fm, "vfm")
            bcol = (bk_c if kv == 0 else bv_c)[:, oc:oc + 1]
            if kv == 0:
                esl = e_scr[:, th * 512:(th + 1) * 512]
                rsl = r_scr[:]
                eres, rres = f"e{th}", "r"
                step("vector", lambda e, PP=PP, bcol=bcol, esl=esl:
                     e.tensor_scalar(esl, PP[:], bcol, 0.0, ALU.add, ALU.min),
                     reads=(ppres, "biases"), writes=(eres,))
                step("scalar", lambda e, esl=esl: e.activation(
                    esl, esl, AF.Exp), reads=(eres,), writes=(eres,))
                step("vector", lambda e, PP=PP, bcol=bcol, rsl=rsl:
                     e.tensor_scalar(rsl, PP[:], bcol, 0.0, ALU.add, ALU.max),
                     reads=(ppres, "biases"), writes=(rres,))
                step("vector", lambda e, fmdst=fmdst, th=th, esl=esl, rsl=rsl:
                     e.tensor_add(fmdst[:, th * 512:(th + 1) * 512], esl, rsl),
                     reads=(eres, rres), writes=(fres,))
            else:
                step("vector", lambda e, PP=PP, bcol=bcol, fmdst=fmdst, th=th:
                     e.tensor_scalar_add(
                         fmdst[:, th * 512:(th + 1) * 512], PP[:], bcol),
                     reads=(ppres, "biases"), writes=(fres,))

        def kv_tail_tr(oc):
            for kv, fmdst, fres in ((0, k_fm, "kfm"), (1, v_fm, "vfm")):
                PTx = PT if kv == 0 else PT2
                pres = "PT" if kv == 0 else "PT2"
                def trk(e, fmdst=fmdst, PTx=PTx):
                    last = None
                    for b in range(8):
                        last = e.transpose(
                            PTx[:, b * 128:(b + 1) * 128],
                            fmdst[:, b * 128:(b + 1) * 128], ident_b[:])
                    return last
                step("tensor", trk, reads=(fres, "ident_b"), writes=(pres,))
                tmdst, tres = (k_tm, "ktm") if kv == 0 else (v_tm, "vtm")
                step("vector", lambda e, tmdst=tmdst, PTx=PTx: e.tensor_copy(
                    tmdst[:], PTx[:]), reads=(pres,), writes=(tres,))

        def kv_tail_mm(oc):
            PK = P2 if oc % 2 == 0 else P3
            pkres = "P2" if oc % 2 == 0 else "P3"
            def kvtmm(e, PK=PK):
                last = None
                for b in range(8):
                    last = e.matmul(
                        PK[:, 0:128], k_tm[:, b * 128:(b + 1) * 128],
                        v_tm[:, b * 128:(b + 1) * 128],
                        start=(b == 0), stop=(b == 7))
                return last
            step("tensor", kvtmm, reads=("ktm", "vtm"), writes=(pkres,))
            def normm(e, PK=PK):
                last = None
                for b in range(8):
                    last = e.matmul(
                        PK[:, 128:129], k_tm[:, b * 128:(b + 1) * 128],
                        ones_c[:], start=(b == 0), stop=(b == 7))
                return last
            step("tensor", normm, reads=("ktm", "ones"), writes=(pkres,))
            def kvev(e, oc=oc, PK=PK):
                return e.tensor_copy(
                    staging[:, oc * 129:oc * 129 + 129], PK[:, 0:129])
            step("vector", kvev, reads=(pkres,),
                 writes=("stagA" if oc < 4 else "stagB",))

        def emit_collective(half):
            sres = "stagA" if half == 0 else "stagB"
            ccin = cc_in1 if half == 0 else cc_in2
            ccout = cc_out1 if half == 0 else cc_out2
            inres, outres = f"ccin{half}", f"ccout{half}"
            sl = staging[:, half * 516:(half + 1) * 516]
            dma("scalar", "xq", ccin, sl, reads=(sres,), writes=(inres,))
            cc_waits = list(_gather(None, (inres,), (outres,)).values())
            cnt["cc"] += 1
            cc_tok = (Scc, cnt["cc"])
            BUILD_LOG.append(("gpsimd", f"collective{half}",
                              [(SEMNAME[id(s)], v) for s, v in cc_waits],
                              [("Scc", cnt["cc"])]))
            ccv = cnt["cc"]
            def fn_cc(e, wl=cc_waits, ccin=ccin, ccout=ccout, ccv=ccv):
                for s, v in wl:
                    e.wait_ge(s, v)
                e.collective_compute(
                    "AllReduce", ALU.add,
                    replica_groups=[[0, 1, 2, 3], [4, 5, 6, 7]],
                    ins=[ccin], outs=[ccout],
                ).then_inc(Scc, 1)
            em("gpsimd", fn_cc)
            _commit(cc_tok, (inres,), (outres,))
            dma("gpsimd", "gq", sl, ccout, reads=(outres,), writes=(sres,))

        for s in range(32):
            oc, kv, th = s // 4, (s % 4) // 2, s % 2
            ci = 2 * oc + kv
            proj_group(P0 if th == 0 else P1, "P0" if th == 0 else "P1", ci,
                       lambda c, th=th: xn_v(c, th * 512, (th + 1) * 512),
                       extra_reads=("xn_fm",))
            if s % 4 == 2:
                cast_chunk(2 * oc + 2)
            if s % 4 == 3:
                cast_chunk(2 * oc + 3)
            if s >= 1:
                kv_epi(s - 1)
            if s % 4 == 0 and oc >= 1:
                kv_tail_tr(oc - 1)
            if s % 4 == 1 and oc >= 1:
                kv_tail_mm(oc - 1)
                if oc - 1 == 3:
                    emit_collective(0)
        kv_epi(31)
        kv_tail_tr(7)
        kv_tail_mm(7)
        emit_collective(1)

        xa_epis = []
        for s in range(4):
            gi, th = s // 2, s % 2
            PP = P0 if th == 0 else P1
            ppres = "P0" if th == 0 else "P1"
            proj_group(PP, ppres, 16 + gi,
                       lambda c, th=th: xn_v(c, th * 512, (th + 1) * 512),
                       extra_reads=("xn_fm",))
            if s == 1:
                cast_chunk(18)
            if s == 3:
                cast_chunk(19)
            if s >= 1:
                xa_epis[s - 1]()
            def mkepi(PP=PP, ppres=ppres, gi=gi, th=th):
                def f():
                    step("vector", lambda e, PP=PP, gi=gi, th=th: e.tensor_copy(
                        xa_bf[:, gi * 1024 + th * 512:
                              gi * 1024 + (th + 1) * 512],
                        PP[:]), reads=(ppres,), writes=("xa",))
                return f
            xa_epis.append(mkepi())

        xa_epis[3]()
        stag_v = staging[:].rearrange("p (h c) -> p h c", c=129)
        kvt_v = kvt_bf[:].rearrange("p (h c) -> p h c", c=128)

        def emit_cc_consumers():
            step("vector", lambda e: e.tensor_copy(kvt_v, stag_v[:, :, 0:128]),
                 reads=("stagA", "stagB"), writes=("kvt",))
            step("vector", lambda e: e.tensor_scalar_add(
                stats[:, 8:16].rearrange("p (h o) -> p h o", o=1),
                stag_v[:, :, 128:129], ATT_EPS),
                reads=("stagA", "stagB"), writes=("stCC",))
            step("vector", lambda e: e.reciprocal(
                stats[:, 16:24], stats[:, 8:16]),
                reads=("stCC",), writes=("stCC",),
                extra=((Sdv, cnt["vector"]),))

        cast_chunk(20)
        first_attn_state = [True]

        def q_phi(k):
            j, th = k // 2, k % 2
            PP = P0 if th == 0 else P1
            ppres = "P0" if th == 0 else "P1"
            bcol = bq_c[:, j:j + 1]
            esl = e_scr[:, th * 512:(th + 1) * 512]
            rsl = r_scr[:]
            eres, rres = f"e{th}", "r"
            qsl = q_bf[:, th * 512:(th + 1) * 512]
            qres = f"q{th}"
            step("vector", lambda e, PP=PP, bcol=bcol, esl=esl:
                 e.tensor_scalar(esl, PP[:], bcol, 0.0, ALU.add, ALU.min),
                 reads=(ppres, "biases"), writes=(eres,))
            step("scalar", lambda e, esl=esl: e.activation(
                esl, esl, AF.Exp), reads=(eres,), writes=(eres,))
            step("vector", lambda e, PP=PP, bcol=bcol, rsl=rsl:
                 e.tensor_scalar(rsl, PP[:], bcol, 0.0, ALU.add, ALU.max),
                 reads=(ppres, "biases"), writes=(rres,))
            step("vector", lambda e, qsl=qsl, esl=esl, rsl=rsl:
                 e.tensor_add(qsl, esl, rsl), reads=(eres, rres), writes=(qres,))

        def attn_mm(k):
            j, th = k // 2, k % 2
            PA = P2 if th == 0 else P3
            pares = "P2" if th == 0 else "P3"
            qsl = q_bf[:, th * 512:(th + 1) * 512]
            gcol = j // 4
            step("tensor", lambda e, PA=PA, qsl=qsl, gcol=gcol: e.matmul(
                PA[:], kvt_bf[:, gcol * 128:(gcol + 1) * 128], qsl,
                start=True, stop=True),
                reads=("kvt", f"q{th}"), writes=(pares,))

        def attn_evac(k):
            j, th = k // 2, k % 2
            PA = P2 if th == 0 else P3
            pares = "P2" if th == 0 else "P3"
            gcol = j // 4
            aw = ("attn",) + tuple(ARENA_P1) if first_attn_state[0] else ("attn",)
            first_attn_state[0] = False
            step("vector", lambda e, PA=PA, j=j, th=th, gcol=gcol:
                 e.tensor_scalar_mul(
                     attn_v(j, th * 512, (th + 1) * 512), PA[:],
                     stats[:, 16 + gcol:17 + gcol]),
                 reads=(pares, "stCC"), writes=aw)

        for k in range(64):
            j, th = k // 2, k % 2
            cq = 20 + j
            proj_group(P0 if th == 0 else P1, "P0" if th == 0 else "P1", cq,
                       lambda c, th=th: xn_v(c, th * 512, (th + 1) * 512),
                       extra_reads=("xn_fm",))
            if j + 1 < HQ:
                cast_half(2 * (cq + 1) + th)
            if k == 1:
                emit_cc_consumers()
            if k >= 1:
                attn_mm(k - 1)
                q_phi(k)
            else:
                q_phi(0)
            if k >= 2:
                attn_evac(k - 2)
        attn_mm(63)
        attn_evac(62)
        attn_evac(63)

        xfm_r = din["x_fm"].rearrange("(r p) t -> r p t", p=128)
        out_r = out_fm.rearrange("(r p) t -> r p t", p=128)

        def issue_xfm(k):
            j, th = k // 2, k % 2
            src = xfm_r[j][:, th * 512:(th + 1) * 512]
            dst = x_scr[:, (k % 2) * 512:((k % 2) + 1) * 512]
            dma("scalar", "xq", dst, src, writes=(f"x{k % 2}",))

        for k in range(2):
            issue_xfm(k)

        def oc_epi(k):
            j, th = k // 2, k % 2
            PS = P0 if th == 0 else P1
            psres = "P0" if th == 0 else "P1"
            PG = P2 if th == 0 else P3
            pgres = "P2" if th == 0 else "P3"
            gsl = g_scr[:, th * 512:(th + 1) * 512]
            gres = f"g{th}"
            step("scalar", lambda e, PG=PG, j=j, gsl=gsl: e.activation(
                gsl, PG[:], AF.Silu, bias=bfb_c[:, j:j + 1], scale=1.0),
                reads=(pgres, "biases"), writes=(gres,))
            step("vector", lambda e, PS=PS, j=j, gsl=gsl:
                 e.scalar_tensor_tensor(gsl, PS[:], bo_c[:, j:j + 1], gsl,
                                        ALU.add, ALU.mult),
                 reads=(psres, "biases", gres), writes=(gres,))
            osl = o_scr[:, (k % 2) * 512:((k % 2) + 1) * 512]
            ores = f"o{k % 2}"
            step("vector", lambda e, osl=osl, gsl=gsl, k=k: e.tensor_add(
                osl, gsl, x_scr[:, (k % 2) * 512:((k % 2) + 1) * 512]),
                reads=(gres, f"x{k % 2}"), writes=(ores,))
            dma("scalar", "od", out_r[j][:, th * 512:(th + 1) * 512],
                osl, reads=(ores,))
            if k + 2 < 64:
                issue_xfm(k + 2)

        cast_chunk(52)
        cast_chunk(53)
        for k in range(64):
            j, th = k // 2, k % 2
            co, cf = 52 + 2 * j, 53 + 2 * j
            PS = P0 if th == 0 else P1
            psres = "P0" if th == 0 else "P1"
            PG = P2 if th == 0 else P3
            pgres = "P2" if th == 0 else "P3"
            proj_group(PS, psres, co,
                       lambda c, th=th: attn_v(c, th * 512, (th + 1) * 512),
                       extra_reads=("attn",))
            def upd(e, PG=PG, th=th, j=j):
                last = None
                for gi in range(2):
                    last = e.matmul(
                        PG[:], bft_w[:, gi * 4096 + j * 128:
                                     gi * 4096 + (j + 1) * 128],
                        xa_bf[:, gi * 1024 + th * 512:
                              gi * 1024 + (th + 1) * 512],
                        start=False, stop=(gi == 1))
                return last
            proj_group(PG, pgres, cf,
                       lambda c, th=th: xn_v(c, th * 512, (th + 1) * 512),
                       extra_mm=upd, extra_reads=("xn_fm", "bft_w", "xa"))
            if k >= 1:
                oc_epi(k - 1)
            if th == 1 and j + 1 < 32:
                cast_chunk(co + 2)
                cast_chunk(cf + 2)
        oc_epi(63)

        od_final = [(QSEM["od"][i], qcnt["od"][i]) for i in range(NQ_Q["od"])
                    if qcnt["od"][i] > 0]
        BUILD_LOG.append(("scalar", "final",
                          [(SEMNAME[id(s)], v) for s, v in od_final], []))
        def fin(e, fl=od_final):
            for s, v in fl:
                e.wait_ge(s, v)
        em("scalar", fin)

        @block.sync
        def _(eng):
            for fn in streams["sync"]:
                fn(eng)

        @block.tensor
        def _(eng):
            for fn in streams["tensor"]:
                fn(eng)

        @block.vector
        def _(eng):
            for fn in streams["vector"]:
                fn(eng)

        @block.scalar
        def _(eng):
            for fn in streams["scalar"]:
                fn(eng)

        @block.gpsimd
        def _(eng):
            for fn in streams["gpsimd"]:
                fn(eng)

    return nc


def _pack_w(w):
    O, F = w.shape
    return np.ascontiguousarray(
        w.reshape(O // 128, 128, F // 128, 128).transpose(0, 3, 2, 1)
        .reshape(O // 128, 128, F))


def kernel(L0_attn_hidden_states, L0_attn_Wq, L0_attn_bq, L0_attn_Wk,
           L0_attn_bk, L0_attn_Wv, L0_attn_bv, L0_attn_Wo, L0_attn_bo,
           L0_ffn_W, L0_ffn_bias, L0_ffn_A, L0_ffn_Bf, L0_ffn_coeffs,
           L0_ffn_scales):
    x = np.ascontiguousarray(np.asarray(L0_attn_hidden_states, dtype=np.float32))
    wq = _pack_w(np.asarray(L0_attn_Wq, np.float32))
    wk = _pack_w(np.asarray(L0_attn_Wk, np.float32))
    wv = _pack_w(np.asarray(L0_attn_Wv, np.float32))
    wo = _pack_w(np.asarray(L0_attn_Wo, np.float32))
    wf = _pack_w(np.asarray(L0_ffn_W, np.float32))
    A = np.asarray(L0_ffn_A, np.float32)
    Bf = np.asarray(L0_ffn_Bf, np.float32)
    apack = np.ascontiguousarray(
        A.reshape(2, 8, 16, 32, 128)
        .transpose(0, 4, 3, 1, 2)
        .reshape(2, 128, 4096))
    bft = np.ascontiguousarray(
        Bf.transpose(0, 2, 1).reshape(2, 128, 4096))

    def bias_cols(b, n):
        return np.ascontiguousarray(np.asarray(b, np.float32).reshape(n, 128).T)

    bq = bias_cols(L0_attn_bq, 32)
    bk = bias_cols(L0_attn_bk, 8)
    bv = bias_cols(L0_attn_bv, 8)
    bo = bias_cols(L0_attn_bo, 32)
    bfb = bias_cols(L0_ffn_bias, 32)
    coeffs = np.asarray(L0_ffn_coeffs, np.float32)
    scales = np.asarray(L0_ffn_scales, np.float32)
    creps = np.ascontiguousarray(
        np.repeat(coeffs.reshape(2, 8), 16, axis=1).reshape(2, 128).T)
    sreps = np.ascontiguousarray(
        np.repeat(scales.reshape(2, 8), 16, axis=1).reshape(2, 128).T)
    ident = np.eye(128, dtype=np.float32)

    x_tok_all = x.reshape(2, 4, 1024, 4096)
    x_fm_all = np.ascontiguousarray(x.transpose(2, 0, 1).reshape(4096, 8192))

    shared = {
        "wq": wq, "wk": wk, "wv": wv, "wo": wo, "wf": wf,
        "apack": apack, "bft": bft, "bq": bq, "bk": bk, "bv": bv,
        "bo": bo, "bfb": bfb, "creps": creps, "sreps": sreps, "ident": ident,
    }
    in_maps = []
    for core in range(8):
        b, c = core // 4, core % 4
        m = dict(shared)
        m["x_tok"] = np.ascontiguousarray(x_tok_all[b, c])
        m["x_fm"] = np.ascontiguousarray(
            x_fm_all[:, (b * 4 + c) * 1024:(b * 4 + c + 1) * 1024])
        in_maps.append(m)

    nc = build_nc()
    res = run_bass_kernel_spmd(nc, in_maps, core_ids=list(range(8)))
    out_fm_full = np.concatenate(
        [res.results[core]["out_fm"] for core in range(8)], axis=1)
    out = out_fm_full.reshape(4096, 2, 4096).transpose(1, 2, 0)
    return np.ascontiguousarray(out)


# version 16 (final)
# speedup vs baseline: 1.0298x; 1.0050x over previous
import numpy as np
from contextlib import ExitStack

import concourse.bass as bass
import concourse.mybir as mybir
from concourse.bass_utils import run_bass_kernel_spmd

BUILD_LOG = []

F32 = mybir.dt.float32
BF16 = mybir.dt.bfloat16
U8 = mybir.dt.uint8
AF = mybir.ActivationFunctionType
ALU = mybir.AluOpType
AX = mybir.AxisListType.X

D = 4096
NCF = 32
NT = 1024
NTT = 8
HQ = 32
HKV = 8
LN_EPS = 1e-5
ATT_EPS = 1e-10

CHUNKS = []
for oc in range(HKV):
    CHUNKS.append(("wk", oc))
    CHUNKS.append(("wv", oc))
CHUNKS.append(("apack", 0))
CHUNKS.append(("apack", 1))
CHUNKS.append(("bft", 0))
CHUNKS.append(("bft", 1))
for j in range(HQ):
    CHUNKS.append(("wq", j))
for j in range(32):
    CHUNKS.append(("wo", j))
    CHUNKS.append(("wf", j))
NCHUNK = len(CHUNKS)

N_LAND = 2
N_HR = 4


def build_nc():
    nc = bass.Bass("TRN2", target_bir_lowering=False, debug=False)

    din = {}
    for name, shape in [
        ("x_tok", [NT, D]), ("x_fm", [D, NT]),
        ("wq", [32, 128, 4096]), ("wk", [8, 128, 4096]), ("wv", [8, 128, 4096]),
        ("wo", [32, 128, 4096]), ("wf", [32, 128, 4096]),
        ("apack", [2, 128, 4096]), ("bft", [2, 128, 4096]),
        ("bq", [128, 32]), ("bk", [128, 8]), ("bv", [128, 8]),
        ("bo", [128, 32]), ("bfb", [128, 32]),
        ("creps", [128, 2]), ("sreps", [128, 2]),
        ("ident", [128, 128]),
    ]:
        din[name] = nc.dram_tensor(name, shape, F32, kind="ExternalInput").ap()
    out_fm = nc.dram_tensor("out_fm", [D, NT], F32, kind="ExternalOutput").ap()
    cc_in1 = nc.dram_tensor("cc_in1", [128, 516], F32).ap()
    cc_out1 = nc.dram_tensor("cc_out1", [128, 516], F32).ap()
    cc_in2 = nc.dram_tensor("cc_in2", [128, 516], F32).ap()
    cc_out2 = nc.dram_tensor("cc_out2", [128, 516], F32).ap()

    with ExitStack() as ctx:
        def sbt(name, shape, dt):
            return ctx.enter_context(nc.sbuf_tensor(name, shape, dt))

        def pst(name, shape, dt):
            return ctx.enter_context(nc.psum_tensor(name, shape, dt))

        xn_fm = sbt("xn_fm", [128, NCF * 1024], BF16)
        ring = sbt("ring", [128, N_HR * 2048], BF16)
        land = sbt("land", [128, N_LAND * 2048], F32)
        bft_w = sbt("bft_w", [128, 2 * 4096], BF16)
        kvt_bf = sbt("kvt_bf", [128, 1024], BF16)
        stats = sbt("stats", [128, 32], F32)
        biases = sbt("biases", [128, 118], F32)
        ident_f = sbt("ident_f", [128, 128], F32)
        ident_b = sbt("ident_b", [128, 128], BF16)
        ones_c = sbt("ones_c", [128, 1], BF16)
        q_bf = sbt("q_bf", [128, 2 * 512], BF16)
        xa_bf = sbt("xa_bf", [128, 2 * 1024], BF16)
        e_scr = sbt("e_scr", [128, 2 * 512], F32)
        r_scr = sbt("r_scr", [128, 512], F32)
        g_scr = sbt("g_scr", [128, 2 * 512], F32)
        staging = sbt("staging", [128, 1032], F32)
        x_scr = sbt("x_scr", [128, 2 * 512], F32)
        o_scr = sbt("o_scr", [128, 2 * 512], F32)
        arena = sbt("arena", [128, 65536], U8)

        xt_sl = [arena[:, 0:16384].bitcast(F32), arena[:, 16384:32768].bitcast(F32)]
        xn_tm2 = [arena[:, 32768:40960].bitcast(BF16),
                  arena[:, 49152:57344].bitcast(BF16)]
        k_fm = arena[:, 40960:43008].bitcast(BF16)
        v_fm = arena[:, 43008:45056].bitcast(BF16)
        k_tm = arena[:, 45056:47104].bitcast(BF16)
        v_tm = arena[:, 47104:49152].bitcast(BF16)
        ARENA_P1 = ["xt0", "xt1", "xn_tm0", "xn_tm1", "kfm", "vfm", "ktm",
                    "vtm"]
        attn_bf = arena[:, 0:65536].bitcast(BF16)

        bq_c = biases[:, 0:32]
        bo_c = biases[:, 32:64]
        bfb_c = biases[:, 64:96]
        bk_c = biases[:, 96:104]
        bv_c = biases[:, 104:112]
        creps_c = biases[:, 112:114]
        sreps_c = biases[:, 114:116]
        w128_c = biases[:, 116:118]

        PT = pst("pt", [128, 1024], BF16)
        PT2 = pst("pt2", [128, 1024], BF16)
        PT3 = pst("pt3", [128, 1024], BF16)
        PT4 = pst("pt4", [128, 1024], BF16)
        P0 = pst("p0", [128, 512], F32)
        P1 = pst("p1", [128, 512], F32)
        P2 = pst("p2", [128, 512], F32)
        P3 = pst("p3", [128, 512], F32)
        PS_NAME = {id(PT): "PT", id(PT2): "PT2", id(P0): "P0", id(P1): "P1",
                   id(P2): "P2", id(P3): "P3"}

        Spe = ctx.enter_context(nc.semaphore("spe"))
        Sdv = ctx.enter_context(nc.semaphore("sdv"))
        Sac = ctx.enter_context(nc.semaphore("sac"))
        Spo = ctx.enter_context(nc.semaphore("spo"))
        Scc = ctx.enter_context(nc.semaphore("scc"))
        NQ_Q = {"wq": 16, "xq": 16, "od": 8, "gq": 4}
        QSEM = {
            q: [ctx.enter_context(nc.semaphore(f"s{q}{i}")) for i in range(n)]
            for q, n in NQ_Q.items()
        }
        QIDX = {"wq": 0, "xq": 0, "od": 0, "gq": 0}

        SEMNAME = {id(Spe): "Spe", id(Sdv): "Sdv", id(Sac): "Sac",
                   id(Spo): "Spo", id(Scc): "Scc"}
        for qn, sl in QSEM.items():
            for i, s in enumerate(sl):
                SEMNAME[id(s)] = f"{qn}{i}"
        BUILD_LOG.clear()

        block = ctx.enter_context(nc.Block())

        streams = {"sync": [], "tensor": [], "vector": [], "scalar": [],
                   "gpsimd": []}
        ENG_SEM = {"tensor": Spe, "vector": Sdv, "scalar": Sac,
                   "gpsimd": Spo}
        cnt = {"tensor": 0, "vector": 0, "scalar": 0, "gpsimd": 0, "cc": 0}
        qcnt = {k: [0] * NQ_Q[k] for k in QSEM}
        TRK = {}

        def em(eng, fn):
            streams[eng].append(fn)

        def _add_wait(waits, tok, own_sem):
            if tok is None:
                return
            sem, val = tok
            k = id(sem)
            if k in waits:
                waits[k] = (sem, max(waits[k][1], val))
            else:
                waits[k] = (sem, val)

        def _gather(own_sem, reads, writes):
            waits = {}
            for r in reads:
                e = TRK.get(r)
                if e:
                    _add_wait(waits, e["w"], own_sem)
            for r in writes:
                e = TRK.get(r)
                if e:
                    _add_wait(waits, e["w"], own_sem)
                    for tok in e["rd"].values():
                        _add_wait(waits, tok, own_sem)
            return waits

        def _commit(tok, reads, writes):
            for r in writes:
                TRK[r] = {"w": tok, "rd": {}}
            for r in reads:
                e = TRK.setdefault(r, {"w": None, "rd": {}})
                k = id(tok[0])
                old = e["rd"].get(k)
                if old is None or tok[1] > old[1]:
                    e["rd"][k] = tok

        def step(eng, builder, reads=(), writes=(), extra=(), inc=1):
            own = ENG_SEM[eng]
            waits = _gather(own, reads, writes)
            for tok in extra:
                _add_wait(waits, tok, None)
            wl = list(waits.values())
            cnt[eng] += inc
            tok = (own, cnt[eng])
            BUILD_LOG.append((eng, f"step r={reads} w={writes}",
                              [(SEMNAME[id(s)], v) for s, v in wl],
                              [(SEMNAME[id(own)], cnt[eng])]))
            def fn(e, wl=wl, builder=builder, own=own, inc=inc):
                for s, v in wl:
                    e.wait_ge(s, v)
                inst = builder(e)
                inst.then_inc(own, inc)
            em(eng, fn)
            _commit(tok, reads, writes)
            return tok

        def dma(eng, qkey, out_ap, in_ap, reads=(), writes=()):
            waits = _gather(None, reads, writes)
            wl = list(waits.values())
            r = QIDX[qkey] % NQ_Q[qkey]
            QIDX[qkey] += 1
            qsem = QSEM[qkey][r]
            qcnt[qkey][r] += 16
            tok = (qsem, qcnt[qkey][r])
            BUILD_LOG.append((eng, f"dma r={reads} w={writes}",
                              [(SEMNAME[id(s)], v) for s, v in wl],
                              [(SEMNAME[id(qsem)], qcnt[qkey][r])]))
            def fn(e, wl=wl, out_ap=out_ap, in_ap=in_ap, qsem=qsem):
                for s, v in wl:
                    e.wait_ge(s, v)
                e.dma_start(out=out_ap, in_=in_ap).then_inc(qsem, 16)
            em(eng, fn)
            _commit(tok, reads, writes)
            return tok

        issued_halves = [0]

        def chunk_src(ci):
            kind, idx = CHUNKS[ci]
            return din[kind][idx]

        def issue_half(hi):
            if hi >= NCHUNK * 2:
                return
            assert hi == issued_halves[0], (hi, issued_halves[0])
            ci, part = hi // 2, hi % 2
            src = chunk_src(ci)[:, part * 2048:(part + 1) * 2048]
            dst = land[:, (hi % N_LAND) * 2048:((hi % N_LAND) + 1) * 2048]
            dma("sync", "wq", dst, src, reads=(), writes=(f"land{hi % N_LAND}",))
            issued_halves[0] += 1

        def cast_half(hi):
            ci, part = hi // 2, hi % 2
            kind, idx = CHUNKS[ci]
            lres = f"land{hi % N_LAND}"
            lsl = land[:, (hi % N_LAND) * 2048:((hi % N_LAND) + 1) * 2048]
            if kind == "bft":
                dst = bft_w[:, idx * 4096 + part * 2048: idx * 4096 + (part + 1) * 2048]
                wcol = w128_c[:, idx:idx + 1]
                step("vector", lambda e, dst=dst, lsl=lsl, wcol=wcol:
                     e.tensor_scalar_mul(dst, lsl, wcol),
                     reads=(lres, "biases"), writes=("bft_w",))
            else:
                dres = f"hr{hi % N_HR}"
                dst = ring[:, (hi % N_HR) * 2048:((hi % N_HR) + 1) * 2048]
                if hi % 2 == 0:
                    step("scalar", lambda e, dst=dst, lsl=lsl: e.copy(dst, lsl),
                         reads=(lres,), writes=(dres,))
                else:
                    step("vector", lambda e, dst=dst, lsl=lsl: e.tensor_copy(dst, lsl),
                         reads=(lres,), writes=(dres,))
            issue_half(hi + N_LAND)

        def cast_chunk(ci):
            cast_half(2 * ci)
            cast_half(2 * ci + 1)

        def half_sl(hi):
            return (ring[:, (hi % N_HR) * 2048:((hi % N_HR) + 1) * 2048],
                    f"hr{hi % N_HR}")

        def proj_group(PP, ppres, ci, rhs_fn, extra_mm=None, extra_reads=()):
            for part in range(2):
                wsl, wres = half_sl(2 * ci + part)
                last_part = part == 1 and extra_mm is None
                def bl(e, wsl=wsl, PP=PP, part=part, rhs_fn=rhs_fn,
                       last_part=last_part, extra_mm=extra_mm):
                    last = None
                    for cc in range(16):
                        c = part * 16 + cc
                        last = e.matmul(
                            PP[:], wsl[:, cc * 128:(cc + 1) * 128], rhs_fn(c),
                            start=(c == 0),
                            stop=(c == 31 and last_part))
                    if part == 1 and extra_mm is not None:
                        last = extra_mm(e)
                    return last
                rd = (wres, ) + tuple(extra_reads) if part == 1 else (wres,) + tuple(extra_reads)
                step("tensor", bl, reads=rd, writes=(ppres,))

        def xn_v(c, t0, t1):
            return xn_fm[:, c * 1024 + t0: c * 1024 + t1]

        def attn_v(c, t0, t1):
            return attn_bf[:, c * 1024 + t0: c * 1024 + t1]

        def issue_xtok(i):
            src = din["x_tok"].rearrange("(n p) d -> n p d", p=128)[i]
            dma("sync", "wq", xt_sl[i % 2], src,
                writes=(f"xt{i % 2}",))

        issue_xtok(0)
        issue_xtok(1)
        dma("scalar", "xq", ident_f[:], din["ident"], writes=("ident_f",))
        bias_toks = []
        for dst, src in [
            (biases[:, 0:32], din["bq"]),
            (biases[:, 32:64], din["bo"]),
            (biases[:, 64:96], din["bfb"]),
            (biases[:, 96:104], din["bk"]),
            (biases[:, 104:112], din["bv"]),
            (biases[:, 112:114], din["creps"]),
            (biases[:, 114:116], din["sreps"]),
        ]:
            bias_toks.append(dma("scalar", "xq", dst, src))
        step("vector", lambda e: e.engine_nop(), extra=tuple(bias_toks),
             writes=("biases",))

        for h in range(N_LAND):
            issue_half(h)

        def init_consts():
            step("vector", lambda e: e.tensor_copy(ident_b[:], ident_f[:]),
                 reads=("ident_f",), writes=("ident_b",))
            step("vector", lambda e: e.tensor_mul(w128_c, creps_c, sreps_c),
                 reads=("biases",), writes=("biases",))
            step("vector", lambda e: e.memset(ones_c[:], 1.0), writes=("ones",))

        def ln_stats(i):
            par = i % 2
            xt = xt_sl[par]
            xres = f"xt{par}"
            xn_tm = xn_tm2[par]
            xnres = f"xn_tm{par}"
            st = stats[:, 24:32] if par else stats[:, 0:8]
            stres = f"st{par}"
            sqres = f"sq{par}"
            step("vector", lambda e, xt=xt, st=st: e.reduce_sum(
                st[:, 0:1], xt, axis=AX), reads=(xres,), writes=(stres,))
            step("scalar", lambda e, xt=xt, st=st, xn_tm=xn_tm: e.activation(
                xn_tm, xt, AF.Square, accum_out=st[:, 1:2]),
                reads=(xres,), writes=(xnres, sqres))
            step("vector", lambda e, st=st: e.tensor_scalar_mul(
                st[:, 2:3], st[:, 0:1], 1.0 / D),
                reads=(stres,), writes=(stres,))
            step("vector", lambda e, st=st: e.tensor_mul(
                st[:, 3:4], st[:, 2:3], st[:, 2:3]),
                reads=(stres,), writes=(stres,))
            step("vector", lambda e, st=st: e.scalar_tensor_tensor(
                st[:, 4:5], st[:, 1:2], 1.0 / D, st[:, 3:4],
                ALU.mult, ALU.subtract), reads=(stres, sqres), writes=(stres,))
            step("vector", lambda e, st=st: e.tensor_scalar_add(
                st[:, 4:5], st[:, 4:5], LN_EPS),
                reads=(stres,), writes=(stres,))
            step("scalar", lambda e, st=st: e.activation(
                st[:, 5:6], st[:, 4:5], AF.Ln),
                reads=(stres,), writes=(stres,))
            step("scalar", lambda e, st=st: e.activation(
                st[:, 6:7], st[:, 5:6], AF.Exp, bias=0.0, scale=-0.5),
                reads=(stres,), writes=(stres,))
            step("vector", lambda e, st=st: e.scalar_tensor_tensor(
                st[:, 7:8], st[:, 2:3], -1.0, st[:, 6:7],
                ALU.mult, ALU.mult), reads=(stres,), writes=(stres,))
            step("vector", lambda e, xt=xt, st=st, xn_tm=xn_tm: e.tensor_scalar(
                xn_tm, xt, st[:, 6:7], st[:, 7:8], ALU.mult, ALU.add),
                reads=(xres, stres), writes=(xnres,))
            if i + 2 < NTT:
                issue_xtok(i + 2)

        def ln_transpose(i):
            par = i % 2
            xn_tm = xn_tm2[par]
            xnres = f"xn_tm{par}"
            for b4 in range(4):
                PTx = (PT, PT2, PT3, PT4)[b4]
                pres = ("PT", "PT2", "PT3", "PT4")[b4]
                def trb(e, b4=b4, PTx=PTx, xn_tm=xn_tm):
                    last = None
                    for c8 in range(8):
                        c = b4 * 8 + c8
                        last = e.transpose(
                            PTx[:, c8 * 128:(c8 + 1) * 128],
                            xn_tm[:, c * 128:(c + 1) * 128], ident_b[:])
                    return last
                step("tensor", trb, reads=(xnres, "ident_b"), writes=(pres,))
                c0 = b4 * 8
                def ev(e, c0=c0, i=i, PTx=PTx):
                    src = PTx[:].rearrange("p (c t) -> p c t", t=128)
                    dst = xn_fm[:].rearrange("p (c t) -> p c t", t=1024)[
                        :, c0:c0 + 8, i * 128:(i + 1) * 128]
                    return e.copy(dst, src)
                step("scalar", ev, reads=(pres,), writes=("xn_fm",))

        ln_stats(0)
        init_consts()
        for i in range(NTT):
            if i + 1 < NTT:
                ln_stats(i + 1)
            ln_transpose(i)
            if i == 1:
                cast_chunk(0)
            if i == 3:
                cast_chunk(1)

        def kv_epi(s):
            oc, kv, th = s // 4, (s % 4) // 2, s % 2
            PP = P0 if th == 0 else P1
            ppres = "P0" if th == 0 else "P1"
            fmdst, fres = (k_fm, "kfm") if kv == 0 else (v_fm, "vfm")
            bcol = (bk_c if kv == 0 else bv_c)[:, oc:oc + 1]
            if kv == 0:
                esl = e_scr[:, th * 512:(th + 1) * 512]
                rsl = r_scr[:]
                eres, rres = f"e{th}", "r"
                step("vector", lambda e, PP=PP, bcol=bcol, esl=esl:
                     e.tensor_scalar(esl, PP[:], bcol, 0.0, ALU.add, ALU.min),
                     reads=(ppres, "biases"), writes=(eres,))
                step("scalar", lambda e, esl=esl: e.activation(
                    esl, esl, AF.Exp), reads=(eres,), writes=(eres,))
                step("vector", lambda e, PP=PP, bcol=bcol, rsl=rsl:
                     e.tensor_scalar(rsl, PP[:], bcol, 0.0, ALU.add, ALU.max),
                     reads=(ppres, "biases"), writes=(rres,))
                step("vector", lambda e, fmdst=fmdst, th=th, esl=esl, rsl=rsl:
                     e.tensor_add(fmdst[:, th * 512:(th + 1) * 512], esl, rsl),
                     reads=(eres, rres), writes=(fres,))
            else:
                step("vector", lambda e, PP=PP, bcol=bcol, fmdst=fmdst, th=th:
                     e.tensor_scalar_add(
                         fmdst[:, th * 512:(th + 1) * 512], PP[:], bcol),
                     reads=(ppres, "biases"), writes=(fres,))

        def kv_tail_tr(oc):
            for kv, fmdst, fres in ((0, k_fm, "kfm"), (1, v_fm, "vfm")):
                PTx = PT if kv == 0 else PT2
                pres = "PT" if kv == 0 else "PT2"
                def trk(e, fmdst=fmdst, PTx=PTx):
                    last = None
                    for b in range(8):
                        last = e.transpose(
                            PTx[:, b * 128:(b + 1) * 128],
                            fmdst[:, b * 128:(b + 1) * 128], ident_b[:])
                    return last
                step("tensor", trk, reads=(fres, "ident_b"), writes=(pres,))
                tmdst, tres = (k_tm, "ktm") if kv == 0 else (v_tm, "vtm")
                step("vector", lambda e, tmdst=tmdst, PTx=PTx: e.tensor_copy(
                    tmdst[:], PTx[:]), reads=(pres,), writes=(tres,))

        def kv_tail_mm(oc):
            PK = P2 if oc % 2 == 0 else P3
            pkres = "P2" if oc % 2 == 0 else "P3"
            def kvtmm(e, PK=PK):
                last = None
                for b in range(8):
                    last = e.matmul(
                        PK[:, 0:128], k_tm[:, b * 128:(b + 1) * 128],
                        v_tm[:, b * 128:(b + 1) * 128],
                        start=(b == 0), stop=(b == 7))
                return last
            step("tensor", kvtmm, reads=("ktm", "vtm"), writes=(pkres,))
            def normm(e, PK=PK):
                last = None
                for b in range(8):
                    last = e.matmul(
                        PK[:, 128:129], k_tm[:, b * 128:(b + 1) * 128],
                        ones_c[:], start=(b == 0), stop=(b == 7))
                return last
            step("tensor", normm, reads=("ktm", "ones"), writes=(pkres,))
            def kvev(e, oc=oc, PK=PK):
                return e.tensor_copy(
                    staging[:, oc * 129:oc * 129 + 129], PK[:, 0:129])
            step("vector", kvev, reads=(pkres,),
                 writes=("stagA" if oc < 4 else "stagB",))

        def emit_collective(half):
            sres = "stagA" if half == 0 else "stagB"
            ccin = cc_in1 if half == 0 else cc_in2
            ccout = cc_out1 if half == 0 else cc_out2
            inres, outres = f"ccin{half}", f"ccout{half}"
            sl = staging[:, half * 516:(half + 1) * 516]
            dma("scalar", "xq", ccin, sl, reads=(sres,), writes=(inres,))
            cc_waits = list(_gather(None, (inres,), (outres,)).values())
            cnt["cc"] += 1
            cc_tok = (Scc, cnt["cc"])
            BUILD_LOG.append(("gpsimd", f"collective{half}",
                              [(SEMNAME[id(s)], v) for s, v in cc_waits],
                              [("Scc", cnt["cc"])]))
            ccv = cnt["cc"]
            def fn_cc(e, wl=cc_waits, ccin=ccin, ccout=ccout, ccv=ccv):
                for s, v in wl:
                    e.wait_ge(s, v)
                e.collective_compute(
                    "AllReduce", ALU.add,
                    replica_groups=[[0, 1, 2, 3], [4, 5, 6, 7]],
                    ins=[ccin], outs=[ccout],
                ).then_inc(Scc, 1)
            em("gpsimd", fn_cc)
            _commit(cc_tok, (inres,), (outres,))
            dma("gpsimd", "gq", sl, ccout, reads=(outres,), writes=(sres,))

        for s in range(32):
            oc, kv, th = s // 4, (s % 4) // 2, s % 2
            ci = 2 * oc + kv
            proj_group(P0 if th == 0 else P1, "P0" if th == 0 else "P1", ci,
                       lambda c, th=th: xn_v(c, th * 512, (th + 1) * 512),
                       extra_reads=("xn_fm",))
            if s % 4 == 2:
                cast_chunk(2 * oc + 2)
            if s % 4 == 3:
                cast_chunk(2 * oc + 3)
            if s >= 1:
                kv_epi(s - 1)
            if s % 4 == 0 and oc >= 1:
                kv_tail_tr(oc - 1)
            if s % 4 == 1 and oc >= 1:
                kv_tail_mm(oc - 1)
                if oc - 1 == 3:
                    emit_collective(0)
        kv_epi(31)
        kv_tail_tr(7)
        kv_tail_mm(7)
        emit_collective(1)

        xa_epis = []
        for s in range(4):
            gi, th = s // 2, s % 2
            PP = P0 if th == 0 else P1
            ppres = "P0" if th == 0 else "P1"
            proj_group(PP, ppres, 16 + gi,
                       lambda c, th=th: xn_v(c, th * 512, (th + 1) * 512),
                       extra_reads=("xn_fm",))
            if s == 1:
                cast_chunk(18)
            if s == 3:
                cast_chunk(19)
            if s >= 1:
                xa_epis[s - 1]()
            def mkepi(PP=PP, ppres=ppres, gi=gi, th=th):
                def f():
                    step("vector", lambda e, PP=PP, gi=gi, th=th: e.tensor_copy(
                        xa_bf[:, gi * 1024 + th * 512:
                              gi * 1024 + (th + 1) * 512],
                        PP[:]), reads=(ppres,), writes=("xa",))
                return f
            xa_epis.append(mkepi())

        xa_epis[3]()
        stag_v = staging[:].rearrange("p (h c) -> p h c", c=129)
        kvt_v = kvt_bf[:].rearrange("p (h c) -> p h c", c=128)

        def emit_cc_consumers():
            step("vector", lambda e: e.tensor_copy(kvt_v, stag_v[:, :, 0:128]),
                 reads=("stagA", "stagB"), writes=("kvt",))
            step("vector", lambda e: e.tensor_scalar_add(
                stats[:, 8:16].rearrange("p (h o) -> p h o", o=1),
                stag_v[:, :, 128:129], ATT_EPS),
                reads=("stagA", "stagB"), writes=("stCC",))
            step("vector", lambda e: e.reciprocal(
                stats[:, 16:24], stats[:, 8:16]),
                reads=("stCC",), writes=("stCC",),
                extra=((Sdv, cnt["vector"]),))

        cast_chunk(20)
        first_attn_state = [True]

        def q_phi(k):
            j, th = k // 2, k % 2
            PP = P0 if th == 0 else P1
            ppres = "P0" if th == 0 else "P1"
            bcol = bq_c[:, j:j + 1]
            esl = e_scr[:, th * 512:(th + 1) * 512]
            rsl = r_scr[:]
            eres, rres = f"e{th}", "r"
            qsl = q_bf[:, th * 512:(th + 1) * 512]
            qres = f"q{th}"
            step("vector", lambda e, PP=PP, bcol=bcol, esl=esl:
                 e.tensor_scalar(esl, PP[:], bcol, 0.0, ALU.add, ALU.min),
                 reads=(ppres, "biases"), writes=(eres,))
            step("scalar", lambda e, esl=esl: e.activation(
                esl, esl, AF.Exp), reads=(eres,), writes=(eres,))
            step("vector", lambda e, PP=PP, bcol=bcol, rsl=rsl:
                 e.tensor_scalar(rsl, PP[:], bcol, 0.0, ALU.add, ALU.max),
                 reads=(ppres, "biases"), writes=(rres,))
            step("vector", lambda e, qsl=qsl, esl=esl, rsl=rsl:
                 e.tensor_add(qsl, esl, rsl), reads=(eres, rres), writes=(qres,))

        def attn_mm(k):
            j, th = k // 2, k % 2
            PA = P2 if th == 0 else P3
            pares = "P2" if th == 0 else "P3"
            qsl = q_bf[:, th * 512:(th + 1) * 512]
            gcol = j // 4
            step("tensor", lambda e, PA=PA, qsl=qsl, gcol=gcol: e.matmul(
                PA[:], kvt_bf[:, gcol * 128:(gcol + 1) * 128], qsl,
                start=True, stop=True),
                reads=("kvt", f"q{th}"), writes=(pares,))

        def attn_evac(k):
            j, th = k // 2, k % 2
            PA = P2 if th == 0 else P3
            pares = "P2" if th == 0 else "P3"
            gcol = j // 4
            aw = ("attn",) + tuple(ARENA_P1) if first_attn_state[0] else ("attn",)
            first_attn_state[0] = False
            step("vector", lambda e, PA=PA, j=j, th=th, gcol=gcol:
                 e.tensor_scalar_mul(
                     attn_v(j, th * 512, (th + 1) * 512), PA[:],
                     stats[:, 16 + gcol:17 + gcol]),
                 reads=(pares, "stCC"), writes=aw)

        for k in range(64):
            j, th = k // 2, k % 2
            cq = 20 + j
            proj_group(P0 if th == 0 else P1, "P0" if th == 0 else "P1", cq,
                       lambda c, th=th: xn_v(c, th * 512, (th + 1) * 512),
                       extra_reads=("xn_fm",))
            if j + 1 < HQ:
                cast_half(2 * (cq + 1) + th)
            if k == 1:
                emit_cc_consumers()
            if k >= 1:
                attn_mm(k - 1)
                q_phi(k)
            else:
                q_phi(0)
            if k >= 2:
                attn_evac(k - 2)
        attn_mm(63)
        attn_evac(62)
        attn_evac(63)

        xfm_r = din["x_fm"].rearrange("(r p) t -> r p t", p=128)
        out_r = out_fm.rearrange("(r p) t -> r p t", p=128)

        def issue_xfm(k):
            j, th = k // 2, k % 2
            src = xfm_r[j][:, th * 512:(th + 1) * 512]
            dst = x_scr[:, (k % 2) * 512:((k % 2) + 1) * 512]
            dma("scalar", "xq", dst, src, writes=(f"x{k % 2}",))

        for k in range(2):
            issue_xfm(k)

        def oc_epi(k):
            j, th = k // 2, k % 2
            PS = P0 if th == 0 else P1
            psres = "P0" if th == 0 else "P1"
            PG = P2 if th == 0 else P3
            pgres = "P2" if th == 0 else "P3"
            gsl = g_scr[:, th * 512:(th + 1) * 512]
            gres = f"g{th}"
            step("scalar", lambda e, PG=PG, j=j, gsl=gsl: e.activation(
                gsl, PG[:], AF.Silu, bias=bfb_c[:, j:j + 1], scale=1.0),
                reads=(pgres, "biases"), writes=(gres,))
            step("vector", lambda e, PS=PS, j=j, gsl=gsl:
                 e.scalar_tensor_tensor(gsl, PS[:], bo_c[:, j:j + 1], gsl,
                                        ALU.add, ALU.mult),
                 reads=(psres, "biases", gres), writes=(gres,))
            osl = o_scr[:, (k % 2) * 512:((k % 2) + 1) * 512]
            ores = f"o{k % 2}"
            step("vector", lambda e, osl=osl, gsl=gsl, k=k: e.tensor_add(
                osl, gsl, x_scr[:, (k % 2) * 512:((k % 2) + 1) * 512]),
                reads=(gres, f"x{k % 2}"), writes=(ores,))
            dma("scalar", "od", out_r[j][:, th * 512:(th + 1) * 512],
                osl, reads=(ores,))
            if k + 2 < 64:
                issue_xfm(k + 2)

        cast_chunk(52)
        cast_chunk(53)
        for k in range(64):
            j, th = k // 2, k % 2
            co, cf = 52 + 2 * j, 53 + 2 * j
            PS = P0 if th == 0 else P1
            psres = "P0" if th == 0 else "P1"
            PG = P2 if th == 0 else P3
            pgres = "P2" if th == 0 else "P3"
            proj_group(PS, psres, co,
                       lambda c, th=th: attn_v(c, th * 512, (th + 1) * 512),
                       extra_reads=("attn",))
            def upd(e, PG=PG, th=th, j=j):
                last = None
                for gi in range(2):
                    last = e.matmul(
                        PG[:], bft_w[:, gi * 4096 + j * 128:
                                     gi * 4096 + (j + 1) * 128],
                        xa_bf[:, gi * 1024 + th * 512:
                              gi * 1024 + (th + 1) * 512],
                        start=False, stop=(gi == 1))
                return last
            proj_group(PG, pgres, cf,
                       lambda c, th=th: xn_v(c, th * 512, (th + 1) * 512),
                       extra_mm=upd, extra_reads=("xn_fm", "bft_w", "xa"))
            if k >= 1:
                oc_epi(k - 1)
            if th == 1 and j + 1 < 32:
                cast_chunk(co + 2)
                cast_chunk(cf + 2)
        oc_epi(63)

        od_final = [(QSEM["od"][i], qcnt["od"][i]) for i in range(NQ_Q["od"])
                    if qcnt["od"][i] > 0]
        BUILD_LOG.append(("scalar", "final",
                          [(SEMNAME[id(s)], v) for s, v in od_final], []))
        def fin(e, fl=od_final):
            for s, v in fl:
                e.wait_ge(s, v)
        em("scalar", fin)

        @block.sync
        def _(eng):
            for fn in streams["sync"]:
                fn(eng)

        @block.tensor
        def _(eng):
            for fn in streams["tensor"]:
                fn(eng)

        @block.vector
        def _(eng):
            for fn in streams["vector"]:
                fn(eng)

        @block.scalar
        def _(eng):
            for fn in streams["scalar"]:
                fn(eng)

        @block.gpsimd
        def _(eng):
            for fn in streams["gpsimd"]:
                fn(eng)

    return nc


def _pack_w(w):
    O, F = w.shape
    return np.ascontiguousarray(
        w.reshape(O // 128, 128, F // 128, 128).transpose(0, 3, 2, 1)
        .reshape(O // 128, 128, F))


def kernel(L0_attn_hidden_states, L0_attn_Wq, L0_attn_bq, L0_attn_Wk,
           L0_attn_bk, L0_attn_Wv, L0_attn_bv, L0_attn_Wo, L0_attn_bo,
           L0_ffn_W, L0_ffn_bias, L0_ffn_A, L0_ffn_Bf, L0_ffn_coeffs,
           L0_ffn_scales):
    x = np.ascontiguousarray(np.asarray(L0_attn_hidden_states, dtype=np.float32))
    wq = _pack_w(np.asarray(L0_attn_Wq, np.float32))
    wk = _pack_w(np.asarray(L0_attn_Wk, np.float32))
    wv = _pack_w(np.asarray(L0_attn_Wv, np.float32))
    wo = _pack_w(np.asarray(L0_attn_Wo, np.float32))
    wf = _pack_w(np.asarray(L0_ffn_W, np.float32))
    A = np.asarray(L0_ffn_A, np.float32)
    Bf = np.asarray(L0_ffn_Bf, np.float32)
    apack = np.ascontiguousarray(
        A.reshape(2, 8, 16, 32, 128)
        .transpose(0, 4, 3, 1, 2)
        .reshape(2, 128, 4096))
    bft = np.ascontiguousarray(
        Bf.transpose(0, 2, 1).reshape(2, 128, 4096))

    def bias_cols(b, n):
        return np.ascontiguousarray(np.asarray(b, np.float32).reshape(n, 128).T)

    bq = bias_cols(L0_attn_bq, 32)
    bk = bias_cols(L0_attn_bk, 8)
    bv = bias_cols(L0_attn_bv, 8)
    bo = bias_cols(L0_attn_bo, 32)
    bfb = bias_cols(L0_ffn_bias, 32)
    coeffs = np.asarray(L0_ffn_coeffs, np.float32)
    scales = np.asarray(L0_ffn_scales, np.float32)
    creps = np.ascontiguousarray(
        np.repeat(coeffs.reshape(2, 8), 16, axis=1).reshape(2, 128).T)
    sreps = np.ascontiguousarray(
        np.repeat(scales.reshape(2, 8), 16, axis=1).reshape(2, 128).T)
    ident = np.eye(128, dtype=np.float32)

    x_tok_all = x.reshape(2, 4, 1024, 4096)
    x_fm_all = np.ascontiguousarray(x.transpose(2, 0, 1).reshape(4096, 8192))

    shared = {
        "wq": wq, "wk": wk, "wv": wv, "wo": wo, "wf": wf,
        "apack": apack, "bft": bft, "bq": bq, "bk": bk, "bv": bv,
        "bo": bo, "bfb": bfb, "creps": creps, "sreps": sreps, "ident": ident,
    }
    in_maps = []
    for core in range(8):
        b, c = core // 4, core % 4
        m = dict(shared)
        m["x_tok"] = np.ascontiguousarray(x_tok_all[b, c])
        m["x_fm"] = np.ascontiguousarray(
            x_fm_all[:, (b * 4 + c) * 1024:(b * 4 + c + 1) * 1024])
        in_maps.append(m)

    nc = build_nc()
    res = run_bass_kernel_spmd(nc, in_maps, core_ids=list(range(8)))
    out_fm_full = np.concatenate(
        [res.results[core]["out_fm"] for core in range(8)], axis=1)
    out = out_fm_full.reshape(4096, 2, 4096).transpose(1, 2, 0)
    return np.ascontiguousarray(out)
